# version 11
# baseline (speedup 1.0000x reference)
from contextlib import ExitStack

import numpy as np
import ml_dtypes
import concourse.bass as bass
import concourse.mybir as mybir
from concourse.bass_utils import run_bass_kernel_spmd

F32 = mybir.dt.float32
BF16 = mybir.dt.bfloat16
I32 = mybir.dt.int32
AF = mybir.ActivationFunctionType
ALU = mybir.AluOpType
AXX = mybir.AxisListType.X

P = 128
DM = 4096
DC = 32
TL = 1024
NH = 32
KB = 32
D1 = 1024
IM = 11008
ICN = 86
ICP = 88
EPS = 1e-5
C_L0 = 4.0
C_L1 = 8.0
SCALE_L0 = float(1.0 / np.sqrt(128.0))
SCALE_L1 = float(1.0 / 32.0)
AWQ_ORDER = [0, 2, 4, 6, 1, 3, 5, 7]
L1FFN_PART = 2
G8 = [list(range(8))]
G4 = [[0, 1, 2, 3], [4, 5, 6, 7]]


class Ctr:
    def __init__(self, stack, nc, name):
        self.h = stack.enter_context(nc.semaphore(name))
        self.n = 0

    def inc(self, inst, k=1):
        inst.then_inc(self.h, k)
        self.n += k
        return inst


class Kb:
    def __init__(self, upto=4, debug=()):
        self.upto = upto
        self.debug = set(debug)
        self.nc = bass.Bass("TRN2", target_bir_lowering=False, debug=False)
        self.st = ExitStack()
        self.live = {}
        self.dbg = {}
        nc = self.nc
        self.sy = nc.sync
        self.sc = nc.scalar
        self.ve = nc.vector
        self.te = nc.tensor
        self.gp = nc.gpsimd
        S = {}
        for name in ["LD", "LDG", "STO", "CV", "EX", "MM", "EV", "CC", "MIS",
                     "DQ", "CONST", "SL0", "SL1", "SL2", "SL3", "RF0", "RF1",
                     "KV0", "KV1", "STK", "STV", "DQS", "DQA", "DQB", "DQC"]:
            S[name] = Ctr(self.st, nc, name)
        self.SL = [S["SL0"], S["SL1"], S["SL2"], S["SL3"]]
        self.RF = [S["RF0"], S["RF1"]]
        self.KV = [S["KV0"], S["KV1"]]
        self.S = S
        self.p_seq = {}
        self.p_ctr = 0
        self._init_arena()

    def _init_arena(self):
        nc = self.nc
        total = nc.sbuf_bytes_remaining - 2048
        self.arena_h = nc.alloc_sbuf_tensor("arena", [P, total],
                                            mybir.dt.uint8)
        base = nc.lookup_mloc(self.arena_h).addr
        self.arena_free = [(base, total)]
        self.sb_regions = {}

    def alloc(self, name, shape, dt):
        size = int(np.prod(shape[1:])) * {F32: 4, BF16: 2, I32: 4}[dt]
        size = (size + 63) // 64 * 64
        for i, (off, sz) in enumerate(self.arena_free):
            if sz >= size:
                self.arena_free[i] = (off + size, sz - size)
                if self.arena_free[i][1] == 0:
                    self.arena_free.pop(i)
                self.sb_regions[name] = (off, size)
                h = self.nc.alloc_sbuf_tensor_at(name, shape, dt, offset=off)
                return h.ap()
        raise MemoryError(f"sbuf arena full allocating {name} ({size}B): "
                          f"free={self.arena_free}")

    def free(self, *names):
        pnames = [n for n in names if n in self.live]
        for n in sorted(pnames, key=lambda n: -self.p_seq[n]):
            self.live.pop(n).__exit__(None, None, None)
            self.p_seq.pop(n)
        for n in names:
            if n in self.sb_regions:
                off, size = self.sb_regions.pop(n)
                self.arena_free.append((off, size))
                self.arena_free.sort()
                merged = []
                for o, s in self.arena_free:
                    if merged and merged[-1][0] + merged[-1][1] == o:
                        merged[-1] = (merged[-1][0], merged[-1][1] + s)
                    else:
                        merged.append((o, s))
                self.arena_free = [tuple(x) for x in merged]

    def palloc(self, name, shape, dt=F32):
        cm = self.nc.psum_tensor(name, shape, dt)
        t = cm.__enter__()
        self.live[name] = cm
        if not hasattr(self, "p_seq"):
            self.p_seq = {}
            self.p_ctr = 0
        self.p_ctr += 1
        self.p_seq[name] = self.p_ctr
        return t

    def din(self, name, shape, dt=F32):
        return self.nc.dram_tensor(name, shape, dt, kind="ExternalInput").ap()

    def dram(self, name, shape, dt):
        return self.nc.dram_tensor(name, shape, dt, kind="Internal").ap()

    def slab_gate(self):
        S = self.S
        for name in ["SL0", "SL1", "SL2", "SL3", "RF0", "RF1", "KV0", "KV1",
                     "LD", "MM", "CV", "EX", "EV", "STO", "STK", "STV", "DQS",
                     "LDG"]:
            self.sy.wait_ge(S[name].h, S[name].n)

    def phase_gate(self):
        S = self.S
        ve = self.ve
        for name in ["LD", "LDG", "STO", "CV", "EX", "MM", "EV", "MIS", "DQ",
                     "CONST", "SL0", "SL1", "SL2", "SL3", "RF0", "RF1",
                     "KV0", "KV1", "STK", "STV", "DQS", "DQA", "DQB", "DQC"]:
            ve.wait_ge(S[name].h, S[name].n)
        S["MIS"].inc(ve.engine_nop())
        g = S["MIS"].n
        for eng in [self.sy, self.sc, self.te, self.gp]:
            eng.wait_ge(S["MIS"].h, g)

    def xsl(self, dc, t0, width):
        h = t0 // 512
        o = t0 % 512
        return self.Xb[h][:, dc * 512 + o: dc * 512 + o + width]

    def build(self):
        self.declare_io()
        self.make_dequant_tasks()
        upto = self.upto
        self.emit_ln(self.R[0], "b0")
        self.prep_transposes()
        self.prep_converts()
        self.emit_l0_attn()
        if upto >= 2:
            self.emit_ln(self.R[1], "b1")
            self.emit_l0_ffn()
        if upto >= 3:
            self.emit_ln(self.R[2], "b2")
            self.emit_l1_attn()
        if upto >= 4:
            self.emit_ln(self.R[3], "b3")
            self.emit_l1_ffn()
        S = self.S
        self.sy.wait_ge(S["STO"].h, S["STO"].n)
        self.sy.wait_ge(S["STK"].h, S["STK"].n)
        self.sy.wait_ge(S["STV"].h, S["STV"].n)
        self.sy.wait_ge(S["DQS"].h, S["DQS"].n)
        self.sy.wait_ge(S["LDG"].h, S["LDG"].n)
        self.sy.wait_ge(S["CC"].h, S["CC"].n)
        self.st.close()
        return self.nc

    def declare_io(self):
        din, dram = self.din, self.dram
        nc = self.nc
        self.x_t = din("x_t", [DM, TL])
        self.w_s = {m: din(m + "_s", [DM, 512]) for m in
                    ["wq", "wk", "wv", "wo", "wb"]}
        self.ffn_bias_pc = din("ffn_bias_pc", [P, DC])
        self.wr = din("wr", [8, DM])
        self.a_flat = din("a_flat", [P, DM])
        self.bm_flat = din("bm_flat", [8 * DM, 16])
        self.wqkv_s = din("wqkv_s", [512, 3072])
        self.wo1_s = din("wo1_s", [P, DM])
        self.bqkv = din("bqkv", [3072])
        self.bqkv_pc = din("bqkv_pc", [P, 24])
        self.bo1_pc = din("bo1_pc", [P, DC])
        self.gq_s = din("gq_s", [512, 1376], I32)
        self.uq_s = din("uq_s", [512, 1376], I32)
        self.dq_s = din("dq_s", [11 * P, 512], I32)
        self.zall = din("zall", [19, 1376], I32)
        self.sall = din("sall", [19, IM])

        self.out_t = nc.dram_tensor("out_t", [DM, TL], F32,
                                    kind="ExternalOutput").ap()
        self.ident = nc.inline_tensor(
            np.eye(P, dtype=ml_dtypes.bfloat16), name="ident").ap()
        self.onescol = nc.inline_tensor(
            np.ones((P, 1), dtype=ml_dtypes.bfloat16), name="onescol").ap()

        self.wT_g = {m: dram(m + "T_g", [8 * P, 4 * 8 * 512], BF16) for m in
                     ["wq", "wk", "wv", "wo", "wb"]}
        self.wT_in = {m: dram(m + "T_in", [P, 4 * 8 * 512], BF16) for m in
                      ["wq", "wk", "wv", "wo", "wb"]}
        self.wqkv_in = dram("wqkv_in", [512, 3072], BF16)
        self.wqkv_g = dram("wqkv_g", [DM, 3072], BF16)
        self.wo1_in = dram("wo1_in", [P, DM], BF16)
        self.wo1_g = dram("wo1_g", [D1, DM], BF16)
        self.gate_in = dram("gate_in", [512, IM], BF16)
        self.gate_g = dram("gate_g", [DM, IM], BF16)
        self.up_in = dram("up_in", [512, IM], BF16)
        self.up_g = dram("up_g", [DM, IM], BF16)
        self.down_in = dram("down_in", [11 * P, DM], BF16)
        self.down_g = dram("down_g", [ICP * P, DM], BF16)
        self.bmT_d = dram("bmT_d", [P, DM], BF16)
        self.zs_d = dram("zs_d", [19, IM], BF16)
        self.sb_d = dram("sb_d", [19, IM], BF16)
        self.arow_d = dram("arow_d", [1, TL], F32)
        self.brow_d = dram("brow_d", [1, TL], F32)
        self.wrt_d = dram("wrt_d", [8, TL], BF16)
        self.l0k_in = dram("l0k_in", [DM, TL], BF16)
        self.l0k_gs = [dram(f"l0k_g{g}", [4 * 8 * P, TL], BF16)
                       for g in range(4)]
        self.l0v_ins = [dram(f"l0v_in{g}", [P, 8 * 8 * 129], BF16)
                        for g in range(4)]
        self.l0v_gs = [dram(f"l0v_g{g}", [4 * P, 8 * 8 * 129], BF16)
                       for g in range(4)]
        self.l1k_in = dram("l1k_in", [D1, TL], BF16)
        self.l1k_g = dram("l1k_g", [4 * D1, TL], BF16)
        self.l1v_in = dram("l1v_in", [TL, D1], BF16)
        self.l1v_g = dram("l1v_g", [4 * TL, D1], BF16)
        R = [self.x_t, dram("R1", [DM, TL], F32), dram("R2", [DM, TL], F32),
             dram("R3", [DM, TL], F32), self.out_t]
        R[self.upto] = self.out_t
        self.R = R

        self.ones_sb = self.alloc("ones_sb", [P, 1], BF16)
        self.id_sb = self.alloc("id_sb", [P, P], BF16)
        self.fc = self.alloc("fc", [P, 4], F32)
        S = self.S
        S["CONST"].inc(self.sy.dma_start(out=self.ones_sb[:], in_=self.onescol), 16)
        S["CONST"].inc(self.sy.dma_start(out=self.id_sb[:], in_=self.ident), 16)
        self.const_ld = S["CONST"].n
        S["CV"].inc(self.ve.memset(self.fc[:, 0:1], EPS))
        S["CV"].inc(self.ve.memset(self.fc[:, 1:2], -C_L0))
        S["CV"].inc(self.ve.memset(self.fc[:, 2:3], -C_L1))
        self.const_cv = S["CV"].n

    def prep_transposes(self):
        sy, sc, ve, te, gp, S = self.sy, self.sc, self.ve, self.te, self.gp, self.S
        wldf = self.alloc("wldf", [P, DC * 512], F32)
        wldb = self.alloc("wldb", [P, DC * 512], BF16)
        tst = self.alloc("tst", [P, 4 * DM], BF16)
        psTr = [self.palloc("psTrA", [P, P], BF16),
                self.palloc("psTrB", [P, P], BF16)]

        cv_done, mm_done, stg_done = [], [], []
        sy.wait_ge(S["CV"].h, self.ln_cv)
        for i, m in enumerate(["wk", "wv", "wq", "wo", "wb"]):
            if i > 0:
                sy.wait_ge(S["CV"].h, cv_done[i - 1])
            S["LD"].inc(sy.dma_start(
                out=wldf[:].rearrange("p (c d) -> p c d", c=DC),
                in_=self.w_s[m].rearrange("(c p) d -> p c d", p=P)), 16)
            ldv = S["LD"].n
            ve.wait_ge(S["LD"].h, ldv)
            if i > 0:
                ve.wait_ge(S["MM"].h, mm_done[i - 1])
            S["CV"].inc(ve.tensor_copy(out=wldb[:], in_=wldf[:]))
            cv_done.append(S["CV"].n)

            te.wait_ge(S["CV"].h, cv_done[i])
            if i == 0:
                te.wait_ge(S["CONST"].h, self.const_ld)
            if i > 0:
                te.wait_ge(S["LDG"].h, stg_done[i - 1])
            for j0 in range(128):
                ec, ds = j0 // 4, j0 % 4
                j = i * 128 + j0
                if j >= 2:
                    te.wait_ge(S["EV"].h, j - 1)
                S["MM"].inc(te.transpose(
                    out=psTr[j % 2][:, :],
                    in_=wldb[:, ec * 512 + ds * P: ec * 512 + (ds + 1) * P],
                    identity=self.id_sb[:]))
                sc.wait_ge(S["MM"].h, S["MM"].n)
                S["EV"].inc(sc.copy(
                    out=tst[:, (ec // 4) * 2048 + ds * 512 + (ec % 4) * P:
                            (ec // 4) * 2048 + ds * 512 + (ec % 4) * P + P],
                    in_=psTr[j % 2][:, :]))
            mm_done.append(S["MM"].n)
            gp.wait_ge(S["EV"].h, S["EV"].n)
            S["LDG"].inc(gp.dma_start(
                out=self.wT_in[m], in_=tst[:]), 16)
            stg_done.append(S["LDG"].n)
            gp.wait_ge(S["LDG"].h, stg_done[i])
            S["CC"].inc(gp.collective_compute(
                "AllGather", ALU.bypass, replica_groups=G8,
                ins=[self.wT_in[m]], outs=[self.wT_g[m]]))
        self.cc_w = {m: i + 1 for i, m in
                     enumerate(["wk", "wv", "wq", "wo", "wb"])}
        self.prep_mm_done = mm_done[-1]
        self.prep_cv = cv_done[-1]
        self.free("psTrA", "psTrB")
        self._prep_bufs = (wldf, wldb)

    def prep_converts(self):
        sy, ve, gp, S = self.sy, self.ve, self.gp, self.S
        wldf, wldb = self._prep_bufs
        sy.wait_ge(S["CV"].h, self.prep_cv)
        S["LD"].inc(sy.dma_start(
            out=wldf[:, 0:4 * 3072].rearrange("p (c d) -> p c d", c=4),
            in_=self.wqkv_s.rearrange("(c p) d -> p c d", p=P)), 16)
        ld1 = S["LD"].n
        ve.wait_ge(S["LD"].h, ld1)
        ve.wait_ge(S["MM"].h, self.prep_mm_done)
        ve.wait_ge(S["LDG"].h, S["LDG"].n)
        S["CV"].inc(ve.tensor_copy(out=wldb[:, 0:4 * 3072],
                                   in_=wldf[:, 0:4 * 3072]))
        cv1 = S["CV"].n
        gp.wait_ge(S["CV"].h, cv1)
        S["LDG"].inc(gp.dma_start(
            out=self.wqkv_in.rearrange("(c p) d -> p c d", p=P),
            in_=wldb[:, 0:4 * 3072].rearrange("p (c d) -> p c d", c=4)), 16)
        gp.wait_ge(S["LDG"].h, S["LDG"].n)
        S["CC"].inc(gp.collective_compute(
            "AllGather", ALU.bypass, replica_groups=G8,
            ins=[self.wqkv_in], outs=[self.wqkv_g]))
        self.cc_wqkv = S["CC"].n
        sy.wait_ge(S["CV"].h, cv1)
        S["LD"].inc(sy.dma_start(out=wldf[:, 0:DM], in_=self.wo1_s), 16)
        ld2 = S["LD"].n
        ve.wait_ge(S["LD"].h, ld2)
        ve.wait_ge(S["LDG"].h, S["LDG"].n)
        S["CV"].inc(ve.tensor_copy(out=wldb[:, 0:DM], in_=wldf[:, 0:DM]))
        cv2 = S["CV"].n
        gp.wait_ge(S["CV"].h, cv2)
        S["LDG"].inc(gp.dma_start(out=self.wo1_in, in_=wldb[:, 0:DM]), 16)
        gp.wait_ge(S["LDG"].h, S["LDG"].n)
        S["CC"].inc(gp.collective_compute(
            "AllGather", ALU.bypass, replica_groups=G8,
            ins=[self.wo1_in], outs=[self.wo1_g]))
        self.cc_wo1 = S["CC"].n
        self.free("wldf", "wldb", "tst")

    def make_dequant_tasks(self):
        tasks = []
        plans = [
            ("g", self.gq_s, self.gate_in, self.gate_g, 4, 344, 0),
            ("u", self.uq_s, self.up_in, self.up_g, 4, 344, 4),
            ("d", self.dq_s, self.down_in, self.down_g, 11, 128, 8),
        ]
        for (tag, src, dst, gout, nch, wq, zb) in plans:
            for k in range(nch):
                for q in range(4):
                    tasks.append((tag, src, dst, gout, nch, wq, zb, k, q))
        self.dq_tasks = tasks
        self.dq_ti = 0
        self._dq_marks = {}
        self._dq_w_sto = {}
        self.cc_gate = self.cc_up = self.cc_down = None

    def emit_dequant_rowprep(self):
        sy, ve, gp, S = self.sy, self.ve, self.gp, self.S
        zl = self.alloc("zl", [19, 1376], I32)
        znq = self.alloc("znq", [19, IM // 4], I32)
        sq = self.alloc("sq", [19, IM // 4], F32)
        sb = self.alloc("sb", [19, IM // 4], BF16)
        zs = self.alloc("zs", [19, IM // 4], BF16)
        S["CONST"].inc(sy.dma_start(out=zl[:], in_=self.zall), 16)
        ve.wait_ge(S["CONST"].h, S["CONST"].n)
        for q4 in range(4):
            cw = IM // 4
            if q4 > 0:
                sy.wait_ge(S["DQ"].h, S["DQ"].n)
            S["RF0"].inc(sy.dma_start(
                out=sq[:], in_=self.sall[:, q4 * cw:(q4 + 1) * cw]), 16)
            ve.wait_ge(S["RF0"].h, S["RF0"].n)
            if q4 > 0:
                ve.wait_ge(S["LDG"].h, S["LDG"].n)
            znr = znq[:].rearrange("x (c j) -> x j c", j=8)
            for j in range(8):
                S["DQ"].inc(ve.tensor_scalar(
                    out=znr[:, AWQ_ORDER[j]:AWQ_ORDER[j] + 1, :],
                    in0=zl[:, q4 * 344:(q4 + 1) * 344],
                    scalar1=4 * j, scalar2=15,
                    op0=ALU.logical_shift_right, op1=ALU.bitwise_and))
            S["DQ"].inc(ve.tensor_copy(out=sb[:], in_=sq[:]))
            ve.wait_ge(S["DQ"].h, S["DQ"].n)
            S["DQ"].inc(ve.tensor_copy(out=zs[:], in_=znq[:]))
            gp.wait_ge(S["DQ"].h, S["DQ"].n)
            S["LDG"].inc(gp.dma_start(
                out=self.sb_d[:, q4 * cw:(q4 + 1) * cw], in_=sb[:]), 16)
            S["LDG"].inc(gp.dma_start(
                out=self.zs_d[:, q4 * cw:(q4 + 1) * cw], in_=zs[:]), 16)
        self.free("zl", "znq", "sq", "sb", "zs")
        self.dq_qraw = self.alloc("dq_qraw", [P, 2 * 344], I32)
        self.dq_nib = self.alloc("dq_nib", [P, 2752], I32)
        self.dq_nibb = self.alloc("dq_nibb", [P, 2752], BF16)
        self.dq_w = self.alloc("dq_w", [P, 2 * 2752], BF16)
        self.dq_zs = self.alloc("dq_zs", [P, 2752], BF16)
        self.dq_sb = self.alloc("dq_sb", [P, 2752], BF16)

    def emit_dequant_task(self):
        if self.dq_ti >= len(self.dq_tasks):
            return False
        (tag, src, dst, gout, nch, wq, zb, k, q) = self.dq_tasks[self.dq_ti]
        ti = self.dq_ti
        self.dq_ti += 1
        ve, gp, sc, S = self.ve, self.gp, self.sc, self.S
        nib = wq * 8
        zrow = zb + k
        dq_ab = S["DQA"] if ti % 2 == 0 else S["DQB"]
        if ti >= 2:
            gp.wait_ge(S["DQ"].h, self._dq_marks[ti - 2])
        dq_ab.inc(gp.dma_start(
            out=self.dq_qraw[:, (ti % 2) * 344:(ti % 2) * 344 + wq],
            in_=src[k * P:(k + 1) * P, q * wq:(q + 1) * wq]), 16)
        if ti >= 1:
            gp.wait_ge(S["DQ"].h, self._dq_marks[ti - 1])
        S["DQC"].inc(gp.dma_start(
            out=self.dq_zs[:, 0:nib],
            in_=self.zs_d[zrow:zrow + 1, q * nib:(q + 1) * nib]
            .partition_broadcast(P)), 16)
        S["DQC"].inc(gp.dma_start(
            out=self.dq_sb[:, 0:nib],
            in_=self.sb_d[zrow:zrow + 1, q * nib:(q + 1) * nib]
            .partition_broadcast(P)), 16)
        gp.wait_ge(dq_ab.h, dq_ab.n)
        gp.wait_ge(S["DQC"].h, S["DQC"].n)
        if ti >= 2:
            gp.wait_ge(S["DQS"].h, S["DQS"].n)
        wsl = self.dq_w[:, (ti % 2) * 2752:(ti % 2) * 2752 + nib]
        nibr = self.dq_nib[:, 0:nib].rearrange("p (c j) -> p j c", j=8)
        ve.wait_ge(dq_ab.h, dq_ab.n)
        if ti >= 1:
            ve.wait_ge(S["DQ"].h, self._dq_marks[ti - 1])
        for j in range(8):
            S["DQ"].inc(ve.tensor_scalar(
                out=nibr[:, AWQ_ORDER[j]:AWQ_ORDER[j] + 1, :],
                in0=self.dq_qraw[:, (ti % 2) * 344:(ti % 2) * 344 + wq],
                scalar1=4 * j, scalar2=15,
                op0=ALU.logical_shift_right, op1=ALU.bitwise_and))
        gp.wait_ge(S["DQ"].h, S["DQ"].n)
        S["DQ"].inc(gp.tensor_copy(out=self.dq_nibb[:, 0:nib],
                                   in_=self.dq_nib[:, 0:nib]))
        gp.wait_ge(S["DQ"].h, S["DQ"].n)
        S["DQ"].inc(gp.tensor_tensor(out=wsl, in0=self.dq_nibb[:, 0:nib],
                                     in1=self.dq_zs[:, 0:nib], op=ALU.subtract))
        gp.wait_ge(S["DQ"].h, S["DQ"].n)
        S["DQ"].inc(gp.tensor_tensor(out=wsl, in0=wsl,
                                     in1=self.dq_sb[:, 0:nib], op=ALU.mult))
        self._dq_marks[ti] = S["DQ"].n
        sc.wait_ge(S["DQ"].h, S["DQ"].n)
        S["DQS"].inc(sc.dma_start(
            out=dst[k * P:(k + 1) * P, q * nib:(q + 1) * nib], in_=wsl), 16)
        self._dq_w_sto[ti] = S["DQS"].n
        if (k == nch - 1) and (q == 3):
            gp.wait_ge(S["DQS"].h, S["DQS"].n)
            S["CC"].inc(gp.collective_compute(
                "AllGather", ALU.bypass, replica_groups=G8,
                ins=[dst], outs=[gout]))
            if tag == "g":
                self.cc_gate = S["CC"].n
            elif tag == "u":
                self.cc_up = S["CC"].n
            else:
                self.cc_down = S["CC"].n
        return True

    def flush_dequant(self):
        while self.emit_dequant_task():
            pass

    def emit_ln(self, rin, tag):
        sy, sc, ve, te, gp, S = self.sy, self.sc, self.ve, self.te, self.gp, self.S
        self.Xb = [self.alloc(f"Xb0{tag}", [P, DC * 512], BF16),
                   self.alloc(f"Xb1{tag}", [P, DC * 512], BF16)]
        self.xb_names = (f"Xb0{tag}", f"Xb1{tag}")
        Rb = self.alloc(f"Rb{tag}", [P, DC * TL], BF16)
        Rf = self.alloc(f"Rf{tag}", [P, 2 * TL], F32)
        Sq = self.alloc(f"Sq{tag}", [P, 2 * TL], BF16)
        ab = self.alloc(f"ab{tag}", [P, 2 * TL], F32)
        rows = self.alloc(f"rows{tag}", [1, 4 * TL], F32)
        psStat = self.palloc(f"psStat{tag}", [1, 4 * 512], F32)

        self.phase_gate()
        cv_marks, sq_marks, mm_marks = [], [], []
        for dc in range(DC):
            rf = self.RF[dc % 2]
            if dc >= 2:
                sy.wait_ge(S["CV"].h, cv_marks[dc - 2])
            rf.inc(sy.dma_start(
                out=Rf[:, (dc % 2) * TL:(dc % 2) * TL + TL],
                in_=rin[dc * P:(dc + 1) * P, :]), 16)
            ve.wait_ge(rf.h, rf.n)
            S["CV"].inc(ve.tensor_copy(
                out=Rb[:, dc * TL:(dc + 1) * TL],
                in_=Rf[:, (dc % 2) * TL:(dc % 2) * TL + TL]))
            cv_marks.append(S["CV"].n)
            sc.wait_ge(S["CV"].h, cv_marks[dc])
            if dc >= 2:
                sc.wait_ge(S["MM"].h, mm_marks[dc - 2])
            S["EX"].inc(sc.activation(
                out=Sq[:, (dc % 2) * TL:(dc % 2) * TL + TL],
                in_=Rb[:, dc * TL:(dc + 1) * TL], func=AF.Square))
            sq_marks.append(S["EX"].n)
            te.wait_ge(S["EX"].h, sq_marks[dc])
            if dc == 0:
                te.wait_ge(S["CONST"].h, self.const_ld)
            for half in range(2):
                S["MM"].inc(te.matmul(
                    out=psStat[:, half * 512:(half + 1) * 512],
                    lhsT=self.ones_sb[:],
                    rhs=Rb[:, dc * TL + half * 512: dc * TL + (half + 1) * 512],
                    start=(dc == 0), stop=(dc == DC - 1)))
                S["MM"].inc(te.matmul(
                    out=psStat[:, 1024 + half * 512:1024 + (half + 1) * 512],
                    lhsT=self.ones_sb[:],
                    rhs=Sq[:, (dc % 2) * TL + half * 512:
                           (dc % 2) * TL + (half + 1) * 512],
                    start=(dc == 0), stop=(dc == DC - 1)))
            mm_marks.append(S["MM"].n)

        mean = rows[:, 0:TL]
        var = rows[:, TL:2 * TL]
        rstd = rows[:, 2 * TL:3 * TL]
        brow = rows[:, 3 * TL:4 * TL]
        ve.wait_ge(S["MM"].h, S["MM"].n)
        S["MIS"].inc(ve.tensor_scalar(out=mean, in0=psStat[:, 0:TL],
                                      scalar1=1.0 / DM, scalar2=None,
                                      op0=ALU.mult))
        ve.wait_ge(S["MIS"].h, S["MIS"].n)
        S["MIS"].inc(ve.scalar_tensor_tensor(
            out=var, in0=mean, scalar=-1.0, in1=mean, op0=ALU.mult,
            op1=ALU.mult))
        ve.wait_ge(S["MIS"].h, S["MIS"].n)
        S["MIS"].inc(ve.scalar_tensor_tensor(
            out=var, in0=psStat[:, TL:2 * TL], scalar=1.0 / DM, in1=var,
            op0=ALU.mult, op1=ALU.add))
        sc.wait_ge(S["MIS"].h, S["MIS"].n)
        sc.wait_ge(S["CV"].h, self.const_cv)
        S["EX"].inc(sc.activation(out=rstd, in_=var, func=AF.Sqrt, bias=self.fc[:1, 0:1]))
        ve.wait_ge(S["EX"].h, S["EX"].n)
        S["MIS"].inc(ve.reciprocal(out=rstd, in_=rstd))
        ve.wait_ge(S["MIS"].h, S["MIS"].n)
        S["MIS"].inc(ve.scalar_tensor_tensor(
            out=brow, in0=mean, scalar=-1.0, in1=rstd, op0=ALU.mult,
            op1=ALU.mult))
        gp.wait_ge(S["MIS"].h, S["MIS"].n)
        S["LDG"].inc(gp.dma_start(out=self.arow_d, in_=rstd), 16)
        S["LDG"].inc(gp.dma_start(out=self.brow_d, in_=brow), 16)
        sy.wait_ge(S["LDG"].h, S["LDG"].n)
        S["LD"].inc(sy.dma_start(
            out=ab[:, 0:TL], in_=self.arow_d.partition_broadcast(P)), 16)
        S["LD"].inc(sy.dma_start(
            out=ab[:, TL:2 * TL], in_=self.brow_d.partition_broadcast(P)), 16)
        abld = S["LD"].n
        ve.wait_ge(S["LD"].h, abld)
        for dc in range(DC):
            for h in range(2):
                S["CV"].inc(ve.tensor_tensor(
                    out=self.Xb[h][:, dc * 512:(dc + 1) * 512],
                    in0=Rb[:, dc * TL + h * 512:dc * TL + (h + 1) * 512],
                    in1=ab[:, h * 512:h * 512 + 512], op=ALU.mult))
                ve.wait_ge(S["CV"].h, S["CV"].n)
                S["CV"].inc(ve.tensor_tensor(
                    out=self.Xb[h][:, dc * 512:(dc + 1) * 512],
                    in0=self.Xb[h][:, dc * 512:(dc + 1) * 512],
                    in1=ab[:, TL + h * 512:TL + h * 512 + 512], op=ALU.add))
                ve.wait_ge(S["CV"].h, S["CV"].n)
        self.ln_cv = S["CV"].n
        self.free(f"Rb{tag}", f"Rf{tag}", f"Sq{tag}", f"ab{tag}",
                  f"rows{tag}", f"psStat{tag}")

    def proj_fm(self, wsrc, eg_list, evac_fn, col0=0, cc_wait=None,
                extra_mm=None, tag="", kdc=DC, tiled=False):
        sy, te, S = self.sy, self.te, self.S
        if tiled:
            nslab = 4
            wsl = self.alloc(f"wsl{tag}", [P, nslab * 2048], BF16)
            self.slab_gate()
            slab_marks = {}
            si = 0
            ev_mark = None
            for gi, eg in enumerate(eg_list):
                for r in range(8):
                    sl = self.SL[si % nslab]
                    if si >= nslab:
                        sy.wait_ge(S["MM"].h, slab_marks[si - nslab])
                    if si == 0 and cc_wait is not None:
                        sy.wait_ge(S["CC"].h, cc_wait)
                    sl.inc(sy.dma_start(
                        out=wsl[:, (si % nslab) * 2048:
                                (si % nslab) * 2048 + 2048],
                        in_=wsrc[r * P:(r + 1) * P,
                                 eg * 2048:(eg + 1) * 2048]), 16)
                    te.wait_ge(sl.h, sl.n)
                    if r == 0:
                        te.wait_ge(S["CV"].h, self.ln_cv)
                        if ev_mark is not None:
                            te.wait_ge(S["EV"].h, ev_mark)
                    for dcl in range(4):
                        dc = r * 4 + dcl
                        for sub in range(4):
                            for th in range(2):
                                S["MM"].inc(te.matmul(
                                    out=self.ps8[:, sub * TL + th * 512:
                                                 sub * TL + (th + 1) * 512],
                                    lhsT=wsl[:, (si % nslab) * 2048 +
                                             dcl * 512 + sub * P:
                                             (si % nslab) * 2048 +
                                             dcl * 512 + (sub + 1) * P],
                                    rhs=self.xsl(dc, th * 512, 512),
                                    start=(dc == 0),
                                    stop=(dc == DC - 1 and extra_mm is None)))
                    slab_marks[si] = S["MM"].n
                    si += 1
                if extra_mm is not None:
                    extra_mm(eg)
                evac_fn(gi, eg, S["MM"].n)
                ev_mark = S["EV"].n
            self.free(f"wsl{tag}")
            return
        nslab = 4
        wsl = self.alloc(f"wsl{tag}", [P, nslab * 512], BF16)
        self.slab_gate()
        slab_marks = {}
        si = 0
        ev_mark = None
        for gi, eg in enumerate(eg_list):
            for dc in range(kdc):
                sl = self.SL[si % nslab]
                if si >= nslab:
                    sy.wait_ge(S["MM"].h, slab_marks[si - nslab])
                if si == 0 and cc_wait is not None:
                    sy.wait_ge(S["CC"].h, cc_wait)
                sl.inc(sy.dma_start(
                    out=wsl[:, (si % nslab) * 512:(si % nslab) * 512 + 512],
                    in_=wsrc[dc * P:(dc + 1) * P,
                             col0 + eg * 512:col0 + (eg + 1) * 512]), 16)
                te.wait_ge(sl.h, sl.n)
                if dc == 0:
                    te.wait_ge(S["CV"].h, self.ln_cv)
                    if ev_mark is not None:
                        te.wait_ge(S["EV"].h, ev_mark)
                for sub in range(4):
                    for th in range(2):
                        S["MM"].inc(te.matmul(
                            out=self.ps8[:, sub * TL + th * 512:
                                         sub * TL + (th + 1) * 512],
                            lhsT=wsl[:, (si % nslab) * 512 + sub * P:
                                     (si % nslab) * 512 + (sub + 1) * P],
                            rhs=self.xsl(dc, th * 512, 512),
                            start=(dc == 0),
                            stop=(dc == kdc - 1 and extra_mm is None)))
                slab_marks[si] = S["MM"].n
                si += 1
            if extra_mm is not None:
                extra_mm(eg)
            evac_fn(gi, eg, S["MM"].n)
            ev_mark = S["EV"].n
        self.free(f"wsl{tag}")

    def resid_evac(self, rprev, rnext, state):
        sy, sc, ve, S = self.sy, self.sc, self.ve, self.S
        rst_marks = state.setdefault("rst_marks", {})
        rin_cv = state.setdefault("rin_cv", {})

        def evac(gi, eg, mmv):
            rf = self.RF[gi % 2]
            for sub in range(4):
                oc = eg * 4 + sub
                if oc >= 4:
                    sy.wait_ge(S["CV"].h, rin_cv[oc - 4])
                rf.inc(sy.dma_start(
                    out=state["Rin"][:, (oc % 4) * TL:(oc % 4) * TL + TL],
                    in_=rprev[oc * P:(oc + 1) * P, :]), 16)
            rin_ld = (rf, rf.n)
            for sub in range(4):
                oc = eg * 4 + sub
                ve.wait_ge(S["MM"].h, mmv)
                ve.wait_ge(rin_ld[0].h, rin_ld[1])
                if oc >= 2:
                    ve.wait_ge(S["STO"].h, S["STO"].n)
                if state.get("bias") is not None:
                    ve.wait_ge(S["CONST"].h, S["CONST"].n)
                    S["CV"].inc(ve.scalar_tensor_tensor(
                        out=state["Rst"][:, (oc % 2) * TL:(oc % 2) * TL + TL],
                        in0=self.ps8[:, sub * TL:(sub + 1) * TL],
                        scalar=state["bias"][:, oc:oc + 1],
                        in1=state["Rin"][:, (oc % 4) * TL:(oc % 4) * TL + TL],
                        op0=ALU.add, op1=ALU.add))
                else:
                    S["CV"].inc(ve.tensor_tensor(
                        out=state["Rst"][:, (oc % 2) * TL:(oc % 2) * TL + TL],
                        in0=self.ps8[:, sub * TL:(sub + 1) * TL],
                        in1=state["Rin"][:, (oc % 4) * TL:(oc % 4) * TL + TL],
                        op=ALU.add))
                rin_cv[oc] = S["CV"].n
                sc.wait_ge(S["CV"].h, S["CV"].n)
                S["STO"].inc(sc.dma_start(
                    out=rnext[oc * P:(oc + 1) * P, :],
                    in_=state["Rst"][:, (oc % 2) * TL:(oc % 2) * TL + TL]), 16)
                rst_marks[oc] = S["STO"].n
                S["EV"].inc(ve.engine_nop())
        return evac

    def emit_l0_attn(self):
        sy, sc, ve, te, gp, S = self.sy, self.sc, self.ve, self.te, self.gp, self.S
        self.ps8 = self.palloc("ps8a", [P, 4 * TL], F32)
        Qb = self.alloc("Qb", [P, NH * TL], BF16)
        kst = self.alloc("kst", [P, 4 * TL], BF16)

        kst_sto = {}

        def evac_k(gi, eg, mmv):
            sc.wait_ge(S["MM"].h, mmv)
            ve.wait_ge(S["MM"].h, mmv)
            sc.wait_ge(S["STK"].h, S["STK"].n)
            ve.wait_ge(S["STK"].h, S["STK"].n)
            for sub in range(4):
                if sub < 2:
                    S["EV"].inc(sc.copy(out=kst[:, sub * TL:(sub + 1) * TL],
                                        in_=self.ps8[:, sub * TL:(sub + 1) * TL]))
                else:
                    S["EV"].inc(ve.tensor_copy(
                        out=kst[:, sub * TL:(sub + 1) * TL],
                        in_=self.ps8[:, sub * TL:(sub + 1) * TL]))
            sc.wait_ge(S["EV"].h, S["EV"].n)
            for sub in range(4):
                ec = eg * 4 + sub
                S["STK"].inc(sc.dma_start(
                    out=self.l0k_in[ec * P:(ec + 1) * P, :],
                    in_=kst[:, sub * TL:(sub + 1) * TL]), 16)
            kst_sto[gi] = S["STK"].n

        self.cc_l0k = {}

        def evac_k_cc(gi, eg, mmv):
            evac_k(gi, eg, mmv)
            if gi % 2 == 1:
                hg = gi // 2
                gp.wait_ge(S["STK"].h, S["STK"].n)
                S["CC"].inc(gp.collective_compute(
                    "AllGather", ALU.bypass, replica_groups=G4,
                    ins=[self.l0k_in[hg * 8 * P:(hg + 1) * 8 * P, :]],
                    outs=[self.l0k_gs[hg]]))
                self.cc_l0k[hg] = S["CC"].n

        self.proj_fm(self.wT_g["wk"], list(range(8)), evac_k_cc,
                     cc_wait=self.cc_w["wk"], tag="k0", tiled=True)

        vst = self.alloc("vst", [P, 2 * 516], BF16)
        vsl = self.alloc("vslab", [P, 4 * 2048], BF16)
        self.slab_gate()
        ve.wait_ge(S["MM"].h, S["MM"].n)
        ve.wait_ge(S["STV"].h, S["STV"].n)
        for s in range(2):
            for sub in range(4):
                S["CV"].inc(ve.memset(
                    vst[:, s * 516 + sub * 129 + P: s * 516 + sub * 129 + P + 1],
                    1.0))
        vst_ones = S["CV"].n
        self.cc_l0v = {}
        vs_marks = {}
        sto_marks = {}
        si = 0
        ev_mark = None
        for eg in range(8):
            for r in range(8):
                sl = self.SL[si % 4]
                if si >= 4:
                    sy.wait_ge(S["MM"].h, vs_marks[si - 4])
                if si == 0:
                    sy.wait_ge(S["CC"].h, self.cc_w["wv"])
                sl.inc(sy.dma_start(
                    out=vsl[:, (si % 4) * 2048:(si % 4) * 2048 + 2048],
                    in_=self.wT_g["wv"][r * P:(r + 1) * P,
                                        eg * 2048:(eg + 1) * 2048]), 16)
                te.wait_ge(sl.h, sl.n)
                if r == 0:
                    te.wait_ge(S["CV"].h, self.ln_cv)
                    if ev_mark is not None:
                        te.wait_ge(S["EV"].h, ev_mark)
                for dcl in range(4):
                    dc = r * 4 + dcl
                    for tc in range(8):
                        S["MM"].inc(te.matmul(
                            out=self.ps8[:, tc * 512:(tc + 1) * 512],
                            lhsT=self.xsl(dc, tc * P, P),
                            rhs=vsl[:, (si % 4) * 2048 + dcl * 512:
                                    (si % 4) * 2048 + (dcl + 1) * 512],
                            start=(dc == 0), stop=(dc == DC - 1)))
                vs_marks[si] = S["MM"].n
                si += 1
            mmv = S["MM"].n
            for tc in range(8):
                ve.wait_ge(S["MM"].h, mmv)
                u = eg * 8 + tc
                if u == 0:
                    ve.wait_ge(S["CV"].h, vst_ones)
                if u >= 2:
                    ve.wait_ge(S["STV"].h, S["STV"].n)
                S["EV"].inc(ve.tensor_copy(
                    out=vst[:, (tc % 2) * 516:(tc % 2) * 516 + 516]
                    .rearrange("p (s c) -> p s c", c=129)[:, :, 0:P],
                    in_=self.ps8[:, tc * 512:(tc + 1) * 512]
                    .rearrange("p (s c) -> p s c", c=P)))
                sc.wait_ge(S["EV"].h, S["EV"].n)
                for sub in range(4):
                    h = eg * 4 + sub
                    hg, hl = h // 8, h % 8
                    S["STV"].inc(sc.dma_start(
                        out=self.l0v_ins[hg][:, (hl * 8 + tc) * 129:
                                             (hl * 8 + tc + 1) * 129],
                        in_=vst[:, (tc % 2) * 516 + sub * 129:
                                (tc % 2) * 516 + (sub + 1) * 129]), 16)
                sto_marks[u] = S["STV"].n
            ev_mark = S["EV"].n
            if eg % 2 == 1:
                hg = eg // 2
                gp.wait_ge(S["STV"].h, S["STV"].n)
                S["CC"].inc(gp.collective_compute(
                    "AllGather", ALU.bypass, replica_groups=G4,
                    ins=[self.l0v_ins[hg]], outs=[self.l0v_gs[hg]]))
                self.cc_l0v[hg] = S["CC"].n
        self.free("vslab", "vst")

        def evac_q(gi, eg, mmv):
            sc.wait_ge(S["MM"].h, mmv)
            ve.wait_ge(S["MM"].h, mmv)
            for sub in range(4):
                ec = eg * 4 + sub
                eng = sc if sub < 2 else ve
                S["EV"].inc(eng.tensor_copy(
                    out=Qb[:, ec * TL:(ec + 1) * TL],
                    in_=self.ps8[:, sub * TL:(sub + 1) * TL]) if eng is ve
                    else eng.copy(out=Qb[:, ec * TL:(ec + 1) * TL],
                                  in_=self.ps8[:, sub * TL:(sub + 1) * TL]))

        self.proj_fm(self.wT_g["wq"], list(range(8)), evac_q,
                     cc_wait=self.cc_w["wq"], tag="q0", tiled=True)
        q_mm = S["MM"].n
        q_ev = S["EV"].n
        self.free("kst", "ps8a")

        self.phase_gate()
        self.emit_dequant_rowprep()

        self.free(*self.xb_names)
        self.phase_gate()
        ATok = self.alloc("ATok", [P, NH * TL], BF16)
        Kh = self.alloc("Kh", [P, 2 * 4096], BF16)
        Vh = self.alloc("Vh", [P, 2 * KB * 129], BF16)
        Pt = self.alloc("Pt", [P, 3 * 512], BF16)
        zin = self.alloc("zin", [P, 2], F32)
        psS = [self.palloc("psS0", [P, 512], F32),
               self.palloc("psS1", [P, 512], F32),
               self.palloc("psS2", [P, 512], F32)]
        psO = [self.palloc(f"psO{i}", [P, 129], F32) for i in range(4)]

        ve.wait_ge(S["MM"].h, q_mm)
        ve.wait_ge(S["EV"].h, q_ev)
        S["CV"].inc(ve.engine_nop())
        vh_cv = S["CV"].n

        kh_ld = {}
        exp_marks = {}
        pv_marks = {}
        norm_marks = {}
        tr_marks = {}
        self._pending_he = None

        for h in range(NH):
            hg = h // 8
            hl = h % 8
            sy.wait_ge(S["CC"].h, max(self.cc_l0k[hg], self.cc_l0v[hg]))
            if h >= 2:
                sy.wait_ge(S["MM"].h, pv_marks[(h - 2, 1, KB - 1)])
            else:
                sy.wait_ge(S["CV"].h, vh_cv)
            kv = self.KV[h % 2]
            for r in range(4):
                kv.inc(sy.dma_start(
                    out=Kh[:, (h % 2) * 4096 + r * TL:
                           (h % 2) * 4096 + (r + 1) * TL],
                    in_=self.l0k_gs[hg][r * 8 * P + hl * P:
                                        r * 8 * P + (hl + 1) * P, :]), 16)
                kv.inc(sy.dma_start(
                    out=Vh[:, (h % 2) * KB * 129 + r * 8 * 129:
                           (h % 2) * KB * 129 + (r + 1) * 8 * 129],
                    in_=self.l0v_gs[hg][r * P:(r + 1) * P,
                                        hl * 8 * 129:(hl + 1) * 8 * 129]), 16)
            kh_ld[h] = (kv, kv.n)

            for qh in range(2):
                hi = h * 2 + qh
                s_marks = {}

                def emit_S(kb):
                    gidx = hi * KB + kb
                    te.wait_ge(kh_ld[h][0].h, kh_ld[h][1])
                    if gidx >= 3:
                        pg = gidx - 3
                        pkey = (pg // (2 * KB), (pg // KB) % 2, pg % KB)
                        te.wait_ge(S["EX"].h, exp_marks[pkey])
                    S["MM"].inc(te.matmul(
                        out=psS[kb % 3][:, :],
                        lhsT=Kh[:, (h % 2) * 4096 + kb * P:
                                (h % 2) * 4096 + (kb + 1) * P],
                        rhs=Qb[:, h * TL + qh * 512: h * TL + (qh + 1) * 512],
                        start=True, stop=True))
                    s_marks[kb] = S["MM"].n

                def emit_exp_pv(kb):
                    gidx = hi * KB + kb
                    sc.wait_ge(S["MM"].h, s_marks[kb])
                    if gidx >= 3:
                        pg = gidx - 3
                        pkey = (pg // (2 * KB), (pg // KB) % 2, pg % KB)
                        sc.wait_ge(S["MM"].h, pv_marks[pkey])
                    S["EX"].inc(sc.activation(
                        out=Pt[:, (kb % 3) * 512:(kb % 3) * 512 + 512],
                        in_=psS[kb % 3][:, :],
                        func=AF.Exp, scale=SCALE_L0, bias=self.fc[:, 1:2]))
                    exp_marks[(h, qh, kb)] = S["EX"].n
                    te.wait_ge(S["EX"].h, exp_marks[(h, qh, kb)])
                    if kb == 0 and hi > 0:
                        pk = (h, 0) if qh == 1 else (h - 1, 1)
                        te.wait_ge(S["CV"].h, norm_marks[pk])
                        te.wait_ge(S["EV"].h, tr_marks[pk])
                    for qs in range(4):
                        S["MM"].inc(te.matmul(
                            out=psO[qs][:, :],
                            lhsT=Pt[:, (kb % 3) * 512 + qs * P:
                                    (kb % 3) * 512 + (qs + 1) * P],
                            rhs=Vh[:, (h % 2) * KB * 129 + kb * 129:
                                   (h % 2) * KB * 129 + (kb + 1) * 129],
                            start=(kb == 0), stop=(kb == KB - 1)))
                    pv_marks[(h, qh, kb)] = S["MM"].n

                emit_S(0)
                if self._pending_he is not None:
                    self._pending_he()
                    self._pending_he = None
                    if qh == 0:
                        self.emit_dequant_task()
                emit_S(1)
                for kb in range(2, KB):
                    emit_S(kb)
                    emit_exp_pv(kb - 2)
                emit_exp_pv(KB - 2)
                emit_exp_pv(KB - 1)

                def headend(h=h, qh=qh):
                    pvv = pv_marks[(h, qh, KB - 1)]
                    for qs in range(4):
                        ve.wait_ge(S["MM"].h, pvv)
                        S["MIS"].inc(ve.reciprocal(
                            out=zin[:, qs % 2:qs % 2 + 1],
                            in_=psO[qs][:, P:P + 1]))
                        ve.wait_ge(S["MIS"].h, S["MIS"].n)
                        S["CV"].inc(ve.tensor_scalar(
                            out=ATok[:, h * TL + qh * 512 + qs * P:
                                     h * TL + qh * 512 + (qs + 1) * P],
                            in0=psO[qs][:, 0:P],
                            scalar1=zin[:, qs % 2:qs % 2 + 1], scalar2=None,
                            op0=ALU.mult))
                        ve.wait_ge(S["CV"].h, S["CV"].n)
                    norm_marks[(h, qh)] = S["CV"].n
                    tr_marks[(h, qh)] = S["EV"].n

                self._pending_he = headend
        self._pending_he()
        self._pending_he = None

        self.free("Kh", "Vh", "Pt", "zin", "psS0", "psS1", "psS2",
                  "psO0", "psO1", "psO2", "psO3", "Qb")
        AT = self.alloc("AT", [P, NH * TL], BF16)
        psT2 = [self.palloc("psT2a", [P, P], BF16),
                self.palloc("psT2b", [P, P], BF16)]
        te.wait_ge(S["CV"].h, S["CV"].n)
        te.wait_ge(S["MM"].h, S["MM"].n)
        tr_ev = {}
        for i in range(NH * 8):
            hh, qq = i // 8, i % 8
            if i >= 2:
                te.wait_ge(*tr_ev[i - 2])
            S["MM"].inc(te.transpose(
                out=psT2[i % 2][:, :],
                in_=ATok[:, hh * TL + qq * P: hh * TL + (qq + 1) * P],
                identity=self.id_sb[:]))
            if i % 2 == 0:
                sc.wait_ge(S["MM"].h, S["MM"].n)
                S["EV"].inc(sc.copy(
                    out=AT[:, hh * TL + qq * P: hh * TL + (qq + 1) * P],
                    in_=psT2[i % 2][:, :]))
                tr_ev[i] = (S["EV"].h, S["EV"].n)
            else:
                ve.wait_ge(S["MM"].h, S["MM"].n)
                S["MIS"].inc(ve.tensor_copy(
                    out=AT[:, hh * TL + qq * P: hh * TL + (qq + 1) * P],
                    in_=psT2[i % 2][:, :]))
                tr_ev[i] = (S["MIS"].h, S["MIS"].n)
        te.wait_ge(S["EV"].h, S["EV"].n)
        te.wait_ge(S["MIS"].h, S["MIS"].n)
        S["MM"].inc(te.nop())
        at_ev_mm = S["MM"].n
        at_ev = S["EV"].n

        self.free("ATok", "psT2a", "psT2b")

        self.phase_gate()
        self.ps8 = self.palloc("ps8o", [P, 4 * TL], F32)
        state = {"Rin": self.alloc("RinA", [P, 4 * TL], F32),
                 "Rst": self.alloc("RstA", [P, 2 * TL], F32),
                 "bias": None}
        ev_fn = self.resid_evac(self.R[0], self.R[1], state)

        orig_xsl = self.xsl
        self.xsl = lambda dc, t0, width: AT[:, dc * TL + t0: dc * TL + t0 + width]
        self.ln_cv = self.S["CV"].n
        te.wait_ge(S["MM"].h, at_ev_mm)

        def ev_dq(gi, eg, mmv):
            ev_fn(gi, eg, mmv)
            for _ in range(3):
                self.emit_dequant_task()

        self.proj_fm(self.wT_g["wo"], list(range(8)), ev_dq,
                     cc_wait=self.cc_w["wo"], tag="o0", tiled=True)
        self.xsl = orig_xsl
        self.free("RinA", "RstA", "AT", "ps8o")

    def emit_l0_ffn(self):
        sy, sc, ve, te, gp, S = self.sy, self.sc, self.ve, self.te, self.gp, self.S
        self.phase_gate()

        aux = self.alloc("aux", [P, 3 * DM], F32)
        auxb = self.alloc("auxb", [P, 3 * DM], BF16)
        wrt_sb = self.alloc("wrt_sb", [P, DC * 8], BF16)
        at_a = self.alloc("at_a", [P, DC * P], BF16)
        bst = self.alloc("bst", [16, DM], BF16)
        psTr = [self.palloc("psTra", [P, P], BF16),
                self.palloc("psTrb", [P, P], BF16)]
        S["LD"].inc(sy.dma_start(out=aux[0:8, 0:DM], in_=self.wr), 16)
        S["LD"].inc(sy.dma_start(out=aux[:, DM:2 * DM], in_=self.a_flat), 16)
        S["LD"].inc(sy.dma_start(
            out=aux[:, 2 * DM:3 * DM].rearrange("p (g r) -> p g r", r=16),
            in_=self.bm_flat.rearrange("(g p) r -> p g r", p=P)), 16)
        ld0 = S["LD"].n
        ve.wait_ge(S["LD"].h, ld0)
        S["CV"].inc(ve.tensor_copy(out=auxb[0:8, 0:DM], in_=aux[0:8, 0:DM]))
        S["CV"].inc(ve.tensor_copy(out=auxb[:, DM:2 * DM], in_=aux[:, DM:2 * DM]))
        S["CV"].inc(ve.tensor_copy(out=auxb[:, 2 * DM:3 * DM],
                                   in_=aux[:, 2 * DM:3 * DM]))
        cv0 = S["CV"].n
        te.wait_ge(S["CV"].h, cv0)
        tev = {}
        ti = 0

        def tr_step(in_ap, out_cb, mp=P):
            nonlocal ti
            if ti >= 2:
                te.wait_ge(S["EV"].h, tev[ti - 2])
            S["MM"].inc(te.transpose(out=psTr[ti % 2][0:mp, 0:P], in_=in_ap,
                                     identity=self.id_sb[:]))
            sc.wait_ge(S["MM"].h, S["MM"].n)
            S["EV"].inc(out_cb(psTr[ti % 2]))
            tev[ti] = S["EV"].n
            ti += 1

        for dc in range(DC):
            nonloc = dc
            if ti >= 2:
                te.wait_ge(S["EV"].h, tev[ti - 2])
            S["MM"].inc(te.transpose(out=psTr[ti % 2][0:P, 0:8],
                                     in_=auxb[0:8, dc * P:(dc + 1) * P],
                                     identity=self.id_sb[0:8, 0:8]))
            sc.wait_ge(S["MM"].h, S["MM"].n)
            S["EV"].inc(sc.copy(out=wrt_sb[:, dc * 8:(dc + 1) * 8],
                                in_=psTr[ti % 2][0:P, 0:8]))
            tev[ti] = S["EV"].n
            ti += 1
        for dc in range(DC):
            tr_step(auxb[:, DM + dc * P:DM + (dc + 1) * P],
                    lambda ps, dc=dc: sc.copy(
                        out=at_a[:, dc * P:(dc + 1) * P], in_=ps[:, :]))
        bst_sto = {}
        for e in range(8):
            for oc in range(DC):
                g = e * DC + oc
                if oc == 0 and e >= 1:
                    sc.wait_ge(S["STO"].h, S["STO"].n)
                tr_step(auxb[:, 2 * DM + g * 16:2 * DM + (g + 1) * 16],
                        lambda ps, oc=oc: sc.copy(
                            out=bst[0:16, oc * P:(oc + 1) * P],
                            in_=ps[0:16, 0:P]), mp=16)
            sc.wait_ge(S["EV"].h, S["EV"].n)
            S["STO"].inc(sc.dma_start(
                out=self.bmT_d[e * 16:(e + 1) * 16, :], in_=bst[0:16, :]), 16)
            bst_sto[e] = S["STO"].n
        self.free("aux", "auxb", "bst", "psTra", "psTrb")
        bmT_sb = self.alloc("bmT_sb", [P, DM], BF16)
        sy.wait_ge(S["STO"].h, S["STO"].n)
        S["CONST"].inc(sy.dma_start(out=bmT_sb[:], in_=self.bmT_d), 16)
        bm_ld = S["CONST"].n

        psL = self.palloc("psL", [P, 8], F32)
        psW2 = [self.palloc("psW2a", [P, P], BF16),
                self.palloc("psW2b", [P, P], BF16)]
        rr = self.alloc("rr", [P, 64], F32)
        wqb = self.alloc("wqb", [P, 2 * 8], BF16)
        wt_sb = self.alloc("wt_sb", [8, TL], BF16)
        wbc = self.alloc("wbc", [P, TL], BF16)
        hwT = self.alloc("hwT", [P, TL], BF16)
        wt_ev = {}
        wt_mm = {}
        rt_mis = 0
        te.wait_ge(S["EV"].h, S["EV"].n)
        for tc in range(8):
            te.wait_ge(S["CV"].h, self.ln_cv)
            if tc > 0:
                te.wait_ge(S["MIS"].h, rt_mis)
            for dc in range(DC):
                S["MM"].inc(te.matmul(
                    out=psL[:, :], lhsT=self.xsl(dc, tc * P, P),
                    rhs=wrt_sb[:, dc * 8:(dc + 1) * 8],
                    start=(dc == 0), stop=(dc == DC - 1)))
            mml = S["MM"].n
            lg = rr[:, 0:8]
            m1 = rr[:, 8:9]
            nm1 = rr[:, 9:10]
            dd = rr[:, 16:24]
            ind = rr[:, 24:32]
            msk = rr[:, 32:40]
            m2 = rr[:, 10:11]
            e1 = rr[:, 40:48]
            ed = rr[:, 11:12]
            r2 = rr[:, 12:13]
            ind2 = rr[:, 48:56]
            w1 = rr[:, 56:64]
            ve.wait_ge(S["MM"].h, mml)
            S["MIS"].inc(ve.tensor_copy(out=lg, in_=psL[:, :]))
            ve.wait_ge(S["MIS"].h, S["MIS"].n)
            S["MIS"].inc(ve.reduce_max(out=m1, in_=lg, axis=AXX))
            ve.wait_ge(S["MIS"].h, S["MIS"].n)
            S["MIS"].inc(ve.tensor_scalar(out=nm1, in0=m1, scalar1=-1.0,
                                          scalar2=None, op0=ALU.mult))
            ve.wait_ge(S["MIS"].h, S["MIS"].n)
            S["MIS"].inc(ve.tensor_scalar(out=dd, in0=lg, scalar1=nm1,
                                          scalar2=None, op0=ALU.add))
            ve.wait_ge(S["MIS"].h, S["MIS"].n)
            S["MIS"].inc(ve.tensor_single_scalar(out=ind, in_=dd, scalar=0.0,
                                                 op=ALU.is_ge))
            ve.wait_ge(S["MIS"].h, S["MIS"].n)
            S["MIS"].inc(ve.scalar_tensor_tensor(
                out=msk, in0=ind, scalar=-1e9, in1=lg, op0=ALU.mult,
                op1=ALU.add))
            ve.wait_ge(S["MIS"].h, S["MIS"].n)
            S["MIS"].inc(ve.reduce_max(out=m2, in_=msk, axis=AXX))
            ve.wait_ge(S["MIS"].h, S["MIS"].n)
            sc.wait_ge(S["MIS"].h, S["MIS"].n)
            S["EX"].inc(sc.activation(out=e1, in_=lg, func=AF.Exp, bias=nm1))
            S["EX"].inc(sc.activation(out=ed, in_=m2, func=AF.Exp, bias=nm1))
            ve.wait_ge(S["EX"].h, S["EX"].n)
            S["MIS"].inc(ve.tensor_single_scalar(out=r2, in_=ed, scalar=1.0,
                                                 op=ALU.add))
            ve.wait_ge(S["MIS"].h, S["MIS"].n)
            S["MIS"].inc(ve.reciprocal(out=r2, in_=r2))
            ve.wait_ge(S["MIS"].h, S["MIS"].n)
            S["MIS"].inc(ve.tensor_single_scalar(out=r2, in_=r2, scalar=2.0,
                                                 op=ALU.mult))
            S["MIS"].inc(ve.tensor_scalar(out=ind2, in0=lg, scalar1=m2,
                                          scalar2=None, op0=ALU.is_ge))
            ve.wait_ge(S["MIS"].h, S["MIS"].n)
            S["MIS"].inc(ve.tensor_tensor(out=w1, in0=e1, in1=ind2,
                                          op=ALU.mult))
            ve.wait_ge(S["MIS"].h, S["MIS"].n)
            if tc >= 2:
                ve.wait_ge(S["MM"].h, wt_mm[tc - 2])
            S["MIS"].inc(ve.tensor_scalar(
                out=wqb[:, (tc % 2) * 8:(tc % 2) * 8 + 8], in0=w1, scalar1=r2,
                scalar2=None, op0=ALU.mult))
            rt_mis = S["MIS"].n
            te.wait_ge(S["MIS"].h, rt_mis)
            if tc >= 2:
                te.wait_ge(S["EV"].h, wt_ev[tc - 2])
            S["MM"].inc(te.transpose(
                out=psW2[tc % 2][0:8, 0:P],
                in_=wqb[:, (tc % 2) * 8:(tc % 2) * 8 + 8],
                identity=self.id_sb[:]))
            wt_mm[tc] = S["MM"].n
            sc.wait_ge(S["MM"].h, S["MM"].n)
            S["EV"].inc(sc.copy(out=wt_sb[0:8, tc * P:(tc + 1) * P],
                                in_=psW2[tc % 2][0:8, 0:P]))
            wt_ev[tc] = S["EV"].n
        gp.wait_ge(S["EV"].h, S["EV"].n)
        S["LDG"].inc(gp.dma_start(out=self.wrt_d, in_=wt_sb[0:8, :]), 16)
        sy.wait_ge(S["LDG"].h, S["LDG"].n)
        for e in range(8):
            S["RF1"].inc(sy.dma_start(
                out=wbc[e * 16:(e + 1) * 16, :],
                in_=self.wrt_d[e:e + 1, :].partition_broadcast(16)), 16)
        wbc_ld = ("RF1", S["RF1"].n)

        psH = self.palloc("psH", [P, TL], F32)
        te.wait_ge(S["CONST"].h, bm_ld)
        for dc in range(DC):
            for th in range(2):
                S["MM"].inc(te.matmul(
                    out=psH[:, th * 512:(th + 1) * 512],
                    lhsT=at_a[:, dc * P:(dc + 1) * P],
                    rhs=self.xsl(dc, th * 512, 512),
                    start=(dc == 0), stop=(dc == DC - 1)))
        h_mm = S["MM"].n
        ve.wait_ge(S["MM"].h, h_mm)
        ve.wait_ge(S[wbc_ld[0]].h, wbc_ld[1])
        S["CV"].inc(ve.tensor_tensor(out=hwT[:], in0=psH[:, :], in1=wbc[:],
                                     op=ALU.mult))
        hw_cv = S["CV"].n
        self.free("rr", "wqb", "psL", "psW2a", "psW2b", "wrt_sb", "psH",
                  "wt_sb", "wbc")

        self.phase_gate()
        bias_sb = self.alloc("bias_sb", [P, DC], F32)
        S["CONST"].inc(sy.dma_start(out=bias_sb[:], in_=self.ffn_bias_pc), 16)
        bias_ld = S["CONST"].n
        self.ps8 = self.palloc("ps8f", [P, 4 * TL], F32)
        state = {"Rin": self.alloc("RinF", [P, 4 * TL], F32),
                 "Rst": self.alloc("RstF", [P, 2 * TL], F32),
                 "bias": bias_sb}

        def extra_lora(eg):
            te.wait_ge(S["CV"].h, hw_cv)
            te.wait_ge(S["CONST"].h, bias_ld)
            for sub in range(4):
                oc = eg * 4 + sub
                for th in range(2):
                    S["MM"].inc(te.matmul(
                        out=self.ps8[:, sub * TL + th * 512:
                                     sub * TL + (th + 1) * 512],
                        lhsT=bmT_sb[:, oc * P:(oc + 1) * P],
                        rhs=hwT[:, th * 512:(th + 1) * 512],
                        start=False, stop=True))

        ev_fn = self.resid_evac(self.R[1], self.R[2], state)

        def ev_dq(gi, eg, mmv):
            ev_fn(gi, eg, mmv)
            for _ in range(3):
                self.emit_dequant_task()

        self.proj_fm(self.wT_g["wb"], list(range(8)), ev_dq,
                     cc_wait=self.cc_w["wb"], extra_mm=extra_lora, tag="b0f",
                     tiled=True)
        self.free("RinF", "RstF", "ps8f", "bias_sb", "bmT_sb", "at_a", "hwT")
        self.free(*self.xb_names)

    def emit_l1_attn(self):
        sy, sc, ve, te, gp, S = self.sy, self.sc, self.ve, self.te, self.gp, self.S
        self.flush_dequant()
        self.free("dq_qraw", "dq_nib", "dq_nibb", "dq_w", "dq_zs", "dq_sb")
        self.phase_gate()
        bq_sb = self.alloc("bq_sb", [P, 24], F32)
        S["CONST"].inc(sy.dma_start(out=bq_sb[:], in_=self.bqkv_pc), 16)
        bq_ld = S["CONST"].n
        self.ps8 = self.palloc("ps81", [P, 4 * TL], F32)
        Qb1 = self.alloc("Qb1", [P, 8 * TL], BF16)
        kst = self.alloc("kst1", [P, 4 * TL], BF16)

        kst_sto = {}

        def evac_k(gi, eg, mmv):
            ve.wait_ge(S["MM"].h, mmv)
            ve.wait_ge(S["CONST"].h, bq_ld)
            if gi >= 1:
                ve.wait_ge(S["STK"].h, S["STK"].n)
            for sub in range(4):
                dq = eg * 4 + sub
                S["EV"].inc(ve.tensor_scalar(
                    out=kst[:, sub * TL:(sub + 1) * TL],
                    in0=self.ps8[:, sub * TL:(sub + 1) * TL],
                    scalar1=bq_sb[:, 8 + dq:9 + dq], scalar2=None,
                    op0=ALU.add))
            sc.wait_ge(S["EV"].h, S["EV"].n)
            for sub in range(4):
                dq = eg * 4 + sub
                S["STK"].inc(sc.dma_start(
                    out=self.l1k_in[dq * P:(dq + 1) * P, :],
                    in_=kst[:, sub * TL:(sub + 1) * TL]), 16)
            kst_sto[gi] = S["STK"].n

        self.proj_fm(self.wqkv_g, [0, 1], evac_k, col0=1024,
                     cc_wait=self.cc_wqkv, tag="k1")
        gp.wait_ge(S["STK"].h, S["STK"].n)
        S["CC"].inc(gp.collective_compute(
            "AllGather", ALU.bypass, replica_groups=G4,
            ins=[self.l1k_in], outs=[self.l1k_g]))
        self.cc_l1k = S["CC"].n

        vst = self.alloc("vst1", [P, 2 * 512], BF16)
        vsl = self.alloc("vslab1", [P, 4 * 512], BF16)
        self.slab_gate()
        bqv = self.alloc("bqv", [P, 2 * 512], F32)
        S["CONST"].inc(sy.dma_start(
            out=bqv[:, 0:512],
            in_=self.bqkv[2048:2560].unsqueeze(0).partition_broadcast(P)), 16)
        S["CONST"].inc(sy.dma_start(
            out=bqv[:, 512:1024],
            in_=self.bqkv[2560:3072].unsqueeze(0).partition_broadcast(P)), 16)
        bqv_ld = S["CONST"].n
        vs_marks = {}
        sto_marks = {}
        si = 0
        ev_mark = None
        for vg in range(2):
            for dc in range(DC):
                sl = self.SL[si % 4]
                if si >= 4:
                    sy.wait_ge(S["MM"].h, vs_marks[si - 4])
                sl.inc(sy.dma_start(
                    out=vsl[:, (si % 4) * 512:(si % 4) * 512 + 512],
                    in_=self.wqkv_g[dc * P:(dc + 1) * P,
                                    2048 + vg * 512:2048 + (vg + 1) * 512]), 16)
                te.wait_ge(sl.h, sl.n)
                if dc == 0:
                    te.wait_ge(S["CV"].h, self.ln_cv)
                    if ev_mark is not None:
                        te.wait_ge(S["EV"].h, ev_mark)
                for tc in range(8):
                    S["MM"].inc(te.matmul(
                        out=self.ps8[:, tc * 512:(tc + 1) * 512],
                        lhsT=self.xsl(dc, tc * P, P),
                        rhs=vsl[:, (si % 4) * 512:(si % 4) * 512 + 512],
                        start=(dc == 0), stop=(dc == DC - 1)))
                vs_marks[si] = S["MM"].n
                si += 1
            mmv = S["MM"].n
            for tc in range(8):
                ve.wait_ge(S["MM"].h, mmv)
                ve.wait_ge(S["CONST"].h, bqv_ld)
                u = vg * 8 + tc
                if u >= 2:
                    ve.wait_ge(S["STV"].h, S["STV"].n)
                S["EV"].inc(ve.tensor_tensor(
                    out=vst[:, (tc % 2) * 512:(tc % 2) * 512 + 512],
                    in0=self.ps8[:, tc * 512:(tc + 1) * 512],
                    in1=bqv[:, vg * 512:(vg + 1) * 512], op=ALU.add))
                sc.wait_ge(S["EV"].h, S["EV"].n)
                S["STV"].inc(sc.dma_start(
                    out=self.l1v_in[tc * P:(tc + 1) * P,
                                    vg * 512:(vg + 1) * 512],
                    in_=vst[:, (tc % 2) * 512:(tc % 2) * 512 + 512]), 16)
                sto_marks[u] = S["STV"].n
            ev_mark = S["EV"].n
        self.free("vslab1", "vst1", "bqv")
        gp.wait_ge(S["STV"].h, S["STV"].n)
        S["CC"].inc(gp.collective_compute(
            "AllGather", ALU.bypass, replica_groups=G4,
            ins=[self.l1v_in], outs=[self.l1v_g]))
        self.cc_l1v = S["CC"].n

        def evac_q(gi, eg, mmv):
            ve.wait_ge(S["MM"].h, mmv)
            ve.wait_ge(S["CONST"].h, bq_ld)
            for sub in range(4):
                dq = eg * 4 + sub
                S["EV"].inc(ve.tensor_scalar(
                    out=Qb1[:, dq * TL:(dq + 1) * TL],
                    in0=self.ps8[:, sub * TL:(sub + 1) * TL],
                    scalar1=bq_sb[:, dq:dq + 1], scalar2=None, op0=ALU.add))

        self.proj_fm(self.wqkv_g, [0, 1], evac_q, col0=0,
                     cc_wait=self.cc_wqkv, tag="q1")
        q_mm = S["MM"].n
        q_ev = S["EV"].n
        self.free("kst1", "ps81")
        self.free(*self.xb_names)

        self.phase_gate()
        Vf = self.alloc("Vf", [P, KB * 1025], BF16)
        Kf = self.alloc("Kf", [P, 8 * 4096], BF16)
        AT1 = self.alloc("AT1", [P, 8 * TL], BF16)
        Pt1 = self.alloc("Pt1", [P, 2 * P], BF16)
        atok1 = self.alloc("atok1", [P, TL], BF16)
        zin1 = self.alloc("zin1", [P, 1], F32)
        psS1 = [self.palloc("psS1a", [P, P], F32),
                self.palloc("psS1b", [P, P], F32)]
        psA = self.palloc("psA", [P, TL], F32)
        psZ = self.palloc("psZ", [P, 1], F32)
        psT1 = [self.palloc("psT1a", [P, P], BF16),
                self.palloc("psT1b", [P, P], BF16)]

        ve.wait_ge(S["MM"].h, q_mm)
        ve.wait_ge(S["EV"].h, q_ev)
        for kb in range(KB):
            S["CV"].inc(ve.memset(
                Vf[:, kb * 1025 + 1024:kb * 1025 + 1025], 1.0))
        vf_cv = S["CV"].n
        sy.wait_ge(S["CC"].h, self.cc_l1v)
        sy.wait_ge(S["CV"].h, vf_cv)
        for r in range(4):
            for dq in range(8):
                S["LD"].inc(sy.dma_start(
                    out=Kf[:, dq * 4096 + r * TL: dq * 4096 + (r + 1) * TL],
                    in_=self.l1k_g[r * D1 + dq * P: r * D1 + (dq + 1) * P, :]),
                    16)
            for kc in range(8):
                kb = r * 8 + kc
                S["LD"].inc(sy.dma_start(
                    out=Vf[:, kb * 1025:kb * 1025 + 1024],
                    in_=self.l1v_g[kb * P:(kb + 1) * P, :]), 16)
        kv_ld = S["LD"].n

        exp1 = {}
        pv1 = {}
        norm1 = {}
        tr1 = {}
        for qs in range(8):
            s1_marks = {}

            def emit_S1(kb):
                gidx = qs * KB + kb
                te.wait_ge(S["LD"].h, kv_ld)
                if gidx >= 2:
                    pg = gidx - 2
                    pkey = (pg // KB, pg % KB)
                    te.wait_ge(S["EX"].h, exp1[pkey])
                for dq in range(8):
                    S["MM"].inc(te.matmul(
                        out=psS1[kb % 2][:, :],
                        lhsT=Kf[:, dq * 4096 + kb * P: dq * 4096 + (kb + 1) * P],
                        rhs=Qb1[:, dq * TL + qs * P: dq * TL + (qs + 1) * P],
                        start=(dq == 0), stop=(dq == 7)))
                s1_marks[kb] = S["MM"].n

            def emit_exp_pv1(kb):
                gidx = qs * KB + kb
                sc.wait_ge(S["MM"].h, s1_marks[kb])
                if gidx >= 2:
                    pg = gidx - 2
                    pkey = (pg // KB, pg % KB)
                    sc.wait_ge(S["MM"].h, pv1[pkey])
                S["EX"].inc(sc.activation(
                    out=Pt1[:, (kb % 2) * P:(kb % 2) * P + P],
                    in_=psS1[kb % 2][:, :],
                    func=AF.Exp, scale=SCALE_L1, bias=self.fc[:, 2:3]))
                exp1[(qs, kb)] = S["EX"].n
                te.wait_ge(S["EX"].h, exp1[(qs, kb)])
                if kb == 0 and qs > 0:
                    te.wait_ge(S["CV"].h, norm1[qs - 1])
                    te.wait_ge(S["EV"].h, tr1[qs - 1])
                for vh in range(2):
                    S["MM"].inc(te.matmul(
                        out=psA[:, vh * 512:(vh + 1) * 512],
                        lhsT=Pt1[:, (kb % 2) * P:(kb % 2) * P + P],
                        rhs=Vf[:, kb * 1025 + vh * 512:
                               kb * 1025 + (vh + 1) * 512],
                        start=(kb == 0), stop=(kb == KB - 1)))
                S["MM"].inc(te.matmul(
                    out=psZ[:, :],
                    lhsT=Pt1[:, (kb % 2) * P:(kb % 2) * P + P],
                    rhs=Vf[:, kb * 1025 + 1024:kb * 1025 + 1025],
                    start=(kb == 0), stop=(kb == KB - 1)))
                pv1[(qs, kb)] = S["MM"].n

            for kb in range(KB):
                emit_S1(kb)
                if kb >= 1:
                    emit_exp_pv1(kb - 1)
            emit_exp_pv1(KB - 1)
            ve.wait_ge(S["MM"].h, pv1[(qs, KB - 1)])
            S["MIS"].inc(ve.reciprocal(out=zin1[:, :], in_=psZ[:, :]))
            ve.wait_ge(S["MIS"].h, S["MIS"].n)
            S["CV"].inc(ve.tensor_scalar(
                out=atok1[:], in0=psA[:, :], scalar1=zin1[:, 0:1],
                scalar2=None, op0=ALU.mult))
            norm1[qs] = S["CV"].n
            for dq in range(8):
                te.wait_ge(S["CV"].h, norm1[qs])
                if dq >= 2:
                    te.wait_ge(S["EV"].h, S["EV"].n)
                S["MM"].inc(te.transpose(
                    out=psT1[dq % 2][:, :],
                    in_=atok1[:, dq * P:(dq + 1) * P],
                    identity=self.id_sb[:]))
                sc.wait_ge(S["MM"].h, S["MM"].n)
                S["EV"].inc(sc.copy(
                    out=AT1[:, dq * TL + qs * P: dq * TL + (qs + 1) * P],
                    in_=psT1[dq % 2][:, :]))
            tr1[qs] = S["EV"].n
        at1_ev = S["EV"].n
        self.free("Kf", "Vf", "Pt1", "atok1", "zin1", "psS1a", "psS1b",
                  "psA", "psZ", "psT1a", "psT1b", "Qb1", "bq_sb")

        self.phase_gate()
        bo_sb = self.alloc("bo_sb", [P, DC], F32)
        S["CONST"].inc(sy.dma_start(out=bo_sb[:], in_=self.bo1_pc), 16)
        self.ps8 = self.palloc("ps8m", [P, 4 * TL], F32)
        state = {"Rin": self.alloc("RinM", [P, 4 * TL], F32),
                 "Rst": self.alloc("RstM", [P, 2 * TL], F32),
                 "bias": bo_sb}
        ev_fn = self.resid_evac(self.R[2], self.R[3], state)
        orig_xsl = self.xsl
        self.xsl = lambda dc, t0, width: AT1[:, dc * TL + t0: dc * TL + t0 + width]
        te.wait_ge(S["EV"].h, at1_ev)
        self.proj_fm(self.wo1_g, list(range(8)), ev_fn,
                     cc_wait=self.cc_wo1, tag="o1", kdc=8)
        self.xsl = orig_xsl
        self.free("RinM", "RstM", "ps8m", "bo_sb", "AT1")

    def emit_l1_ffn(self):
        sy, sc, ve, te, gp, S = self.sy, self.sc, self.ve, self.te, self.gp, self.S
        self.phase_gate()
        psG = self.palloc("psG", [P, 2 * 512], F32)
        psU = self.palloc("psU", [P, 2 * 512], F32)
        psD = self.palloc("psD", [P, 2 * 512], F32)
        sg = self.alloc("sg", [P, 2 * 512], F32)

        gu_marks = {}
        h_marks = {}
        if L1FFN_PART == 0:
            self.free("sg", "psG", "psU", "psD")
            return
        for th in range(2):
            Hh = self.alloc(f"Hh{th}", [P, ICN * 512], BF16)
            gsl = self.alloc(f"gsl{th}", [P, 2 * DC * P], BF16)
            usl = self.alloc(f"usl{th}", [P, 2 * DC * P], BF16)
            self.slab_gate()
            for ic in range(ICN):
                u = th * ICN + ic
                if u >= 2:
                    sy.wait_ge(S["MM"].h, gu_marks[u - 2])
                if u == 0:
                    sy.wait_ge(S["CC"].h, self.cc_gate)
                    sy.wait_ge(S["CC"].h, self.cc_up)
                slg = self.SL[ic % 2]
                slu = self.SL[2 + ic % 2]
                slg.inc(sy.dma_start(
                    out=gsl[:, (ic % 2) * DC * P:(ic % 2) * DC * P + DC * P]
                    .rearrange("p (c i) -> p c i", c=DC),
                    in_=self.gate_g[:, ic * P:(ic + 1) * P]
                    .rearrange("(c p) i -> p c i", p=P)), 16)
                slu.inc(sy.dma_start(
                    out=usl[:, (ic % 2) * DC * P:(ic % 2) * DC * P + DC * P]
                    .rearrange("p (c i) -> p c i", c=DC),
                    in_=self.up_g[:, ic * P:(ic + 1) * P]
                    .rearrange("(c p) i -> p c i", p=P)), 16)
                te.wait_ge(slg.h, slg.n)
                te.wait_ge(slu.h, slu.n)
                te.wait_ge(S["CV"].h, self.ln_cv)
                if u >= 2:
                    te.wait_ge(S["CV"].h, h_marks[u - 2])
                for dc in range(DC):
                    S["MM"].inc(te.matmul(
                        out=psG[:, (ic % 2) * 512:(ic % 2) * 512 + 512],
                        lhsT=gsl[:, (ic % 2) * DC * P + dc * P:
                                 (ic % 2) * DC * P + (dc + 1) * P],
                        rhs=self.xsl(dc, th * 512, 512),
                        start=(dc == 0), stop=(dc == DC - 1)))
                    S["MM"].inc(te.matmul(
                        out=psU[:, (ic % 2) * 512:(ic % 2) * 512 + 512],
                        lhsT=usl[:, (ic % 2) * DC * P + dc * P:
                                 (ic % 2) * DC * P + (dc + 1) * P],
                        rhs=self.xsl(dc, th * 512, 512),
                        start=(dc == 0), stop=(dc == DC - 1)))
                gu_marks[u] = S["MM"].n
                sc.wait_ge(S["MM"].h, gu_marks[u])
                if u >= 2:
                    sc.wait_ge(S["CV"].h, h_marks[u - 2])
                S["EX"].inc(sc.activation(
                    out=sg[:, (ic % 2) * 512:(ic % 2) * 512 + 512],
                    in_=psG[:, (ic % 2) * 512:(ic % 2) * 512 + 512],
                    func=AF.Silu))
                ve.wait_ge(S["EX"].h, S["EX"].n)
                S["CV"].inc(ve.tensor_tensor(
                    out=Hh[:, ic * 512:(ic + 1) * 512],
                    in0=sg[:, (ic % 2) * 512:(ic % 2) * 512 + 512],
                    in1=psU[:, (ic % 2) * 512:(ic % 2) * 512 + 512],
                    op=ALU.mult))
                h_marks[u] = S["CV"].n
            h_done = S["CV"].n
            if L1FFN_PART == 5 and th == 0:
                self.free(f"Hh{th}", f"gsl{th}", f"usl{th}")
                self.free(self.xb_names[th])
                self.phase_gate()
                continue
            if L1FFN_PART in (3, 4) and th == 1:
                self.free(f"Hh{th}", f"gsl{th}", f"usl{th}")
                self.free(self.xb_names[th])
                self.phase_gate()
                continue
            if L1FFN_PART < 2:
                self.free(f"Hh{th}", f"gsl{th}", f"usl{th}")
                self.free(self.xb_names[th])
                self.phase_gate()
                continue

            self.free(self.xb_names[th], f"gsl{th}", f"usl{th}")
            self.phase_gate()
            dsl = self.alloc(f"dsl{th}", [P, 2 * ICP * P], BF16)
            Rin = self.alloc(f"RinD{th}", [P, 2 * 512], F32)
            self.slab_gate()
            Rst = self.alloc(f"RstD{th}", [P, 2 * 512], F32)
            dn_marks = {}
            rst_marks = {}
            noc = 4 if L1FFN_PART == 3 else DC
            if L1FFN_PART == 4:
                noc = DC
            for oc in range(noc):
                if oc >= 2:
                    sy.wait_ge(S["MM"].h, dn_marks[oc - 2])
                if oc == 0 and th == 0:
                    sy.wait_ge(S["CC"].h, self.cc_down)
                    sy.wait_ge(S["CV"].h, h_done)
                sld = self.SL[oc % 2]
                for r in range(8):
                    sld.inc(sy.dma_start(
                        out=dsl[:, (oc % 2) * ICP * P + r * 11 * P:
                                (oc % 2) * ICP * P + (r + 1) * 11 * P]
                        .rearrange("p (c i) -> p c i", c=11),
                        in_=self.down_g[r * 11 * P:(r + 1) * 11 * P,
                                        oc * P:(oc + 1) * P]
                        .rearrange("(c p) i -> p c i", p=P)), 16)
                if oc >= 2:
                    sy.wait_ge(S["CV"].h, rst_marks.get(oc - 2, (None, 0))[1])
                rfs = self.RF[oc % 2]
                rfs.inc(sy.dma_start(
                    out=Rin[:, (oc % 2) * 512:(oc % 2) * 512 + 512],
                    in_=self.R[3][oc * P:(oc + 1) * P, th * 512:(th + 1) * 512]),
                    16)
                te.wait_ge(sld.h, sld.n)
                te.wait_ge(S["CV"].h, h_done)
                if oc >= 2:
                    te.wait_ge(S["EV"].h, dn_ev[oc - 2])
                for k in range(ICN):
                    col = ((k % 8) * 11 + (k // 8)) * P
                    S["MM"].inc(te.matmul(
                        out=psD[:, (oc % 2) * 512:(oc % 2) * 512 + 512],
                        lhsT=dsl[:, (oc % 2) * ICP * P + col:
                                 (oc % 2) * ICP * P + col + P],
                        rhs=Hh[:, k * 512:(k + 1) * 512],
                        start=(k == 0), stop=(k == ICN - 1)))
                dn_marks[oc] = S["MM"].n
                ve.wait_ge(S["MM"].h, dn_marks[oc])
                ve.wait_ge(rfs.h, rfs.n)
                if oc >= 2:
                    ve.wait_ge(S["STO"].h, S["STO"].n)
                S["CV"].inc(ve.tensor_tensor(
                    out=Rst[:, (oc % 2) * 512:(oc % 2) * 512 + 512],
                    in0=psD[:, (oc % 2) * 512:(oc % 2) * 512 + 512],
                    in1=Rin[:, (oc % 2) * 512:(oc % 2) * 512 + 512],
                    op=ALU.add))
                if oc == 0:
                    dn_ev = {}
                S["EV"].inc(ve.engine_nop())
                dn_ev[oc] = S["EV"].n
                sc.wait_ge(S["CV"].h, S["CV"].n)
                S["STO"].inc(sc.dma_start(
                    out=self.R[4][oc * P:(oc + 1) * P, th * 512:(th + 1) * 512],
                    in_=Rst[:, (oc % 2) * 512:(oc % 2) * 512 + 512]), 16)
                rst_marks[oc] = (S["STO"].n, S["CV"].n)
            self.free(f"Hh{th}", f"dsl{th}", f"RinD{th}", f"RstD{th}")
            self.phase_gate()
        self.free("sg", "psG", "psU", "psD")


def make_in_maps(inputs):
    asc = np.ascontiguousarray
    x = inputs["L0_attn_x"]
    maps = []
    for c in range(8):
        b, l = c // 4, c % 4
        cl = [min(k, ICN - 1) for k in range(ICP) if k % 8 == c]
        zg = inputs["L1_ffn_gate_qzeros"][4 * c:4 * (c + 1)]
        zu = inputs["L1_ffn_up_qzeros"][4 * c:4 * (c + 1)]
        zd = np.stack([inputs["L1_ffn_down_qzeros"][k] for k in cl], axis=0)
        zdp = np.zeros((11, 1376), np.int32)
        zdp[:, :512] = zd
        zall = np.concatenate([zg, zu, zdp], axis=0).astype(np.int32)
        sg = inputs["L1_ffn_gate_scales"][4 * c:4 * (c + 1)]
        su = inputs["L1_ffn_up_scales"][4 * c:4 * (c + 1)]
        sd = np.stack([inputs["L1_ffn_down_scales"][k] for k in cl], axis=0)
        sdp = np.zeros((11, IM), np.float32)
        sdp[:, :DM] = sd
        sall = np.concatenate([sg, su, sdp], axis=0).astype(np.float32)
        m = {
            "x_t": asc(x[b, l * TL:(l + 1) * TL, :].T),
            "wq_s": asc(inputs["L0_attn_w_q"][:, 512 * c:512 * (c + 1)]),
            "wk_s": asc(inputs["L0_attn_w_k"][:, 512 * c:512 * (c + 1)]),
            "wv_s": asc(inputs["L0_attn_w_v"][:, 512 * c:512 * (c + 1)]),
            "wo_s": asc(inputs["L0_attn_w_o"][:, 512 * c:512 * (c + 1)]),
            "wb_s": asc(inputs["L0_ffn_Wb"][:, 512 * c:512 * (c + 1)]),
            "ffn_bias_pc": asc(inputs["L0_ffn_bias"].reshape(DC, P).T),
            "wr": asc(inputs["L0_ffn_Wr"]),
            "a_flat": asc(inputs["L0_ffn_A"].reshape(P, DM)),
            "bm_flat": asc(inputs["L0_ffn_Bm"].reshape(8 * DM, 16)),
            "wqkv_s": asc(inputs["L1_attn_Wqkv"][512 * c:512 * (c + 1), :]),
            "wo1_s": asc(inputs["L1_attn_Wo"][P * c:P * (c + 1), :]),
            "bqkv": asc(inputs["L1_attn_bqkv"]),
            "bqkv_pc": asc(inputs["L1_attn_bqkv"].reshape(24, P).T),
            "bo1_pc": asc(inputs["L1_attn_bo"].reshape(DC, P).T),
            "gq_s": asc(inputs["L1_ffn_gate_qweight"][512 * c:512 * (c + 1), :]),
            "uq_s": asc(inputs["L1_ffn_up_qweight"][512 * c:512 * (c + 1), :]),
            "dq_s": asc(np.concatenate(
                [inputs["L1_ffn_down_qweight"][P * k:P * (k + 1), :]
                 for k in cl], axis=0)),
            "zall": asc(zall),
            "sall": asc(sall),
        }
        maps.append(m)
    return maps


def kernel(**inputs):
    b = Kb(upto=4)
    nc = b.build()
    in_maps = make_in_maps(inputs)
    res = run_bass_kernel_spmd(nc, in_maps, core_ids=list(range(8)))
    outs = [np.ascontiguousarray(res.results[c]["out_t"].T) for c in range(8)]
    full = np.stack([np.concatenate(outs[0:4], axis=0),
                     np.concatenate(outs[4:8], axis=0)], axis=0)
    return full.astype(np.float32)


# version 12
# speedup vs baseline: 1.0433x; 1.0433x over previous
from contextlib import ExitStack

import numpy as np
import ml_dtypes
import concourse.bass as bass
import concourse.mybir as mybir
from concourse.bass_utils import run_bass_kernel_spmd

F32 = mybir.dt.float32
BF16 = mybir.dt.bfloat16
I32 = mybir.dt.int32
AF = mybir.ActivationFunctionType
ALU = mybir.AluOpType
AXX = mybir.AxisListType.X

P = 128
DM = 4096
DC = 32
TL = 1024
NH = 32
KB = 32
D1 = 1024
IM = 11008
ICN = 86
ICP = 88
EPS = 1e-5
C_L0 = 4.0
C_L1 = 8.0
SCALE_L0 = float(1.0 / np.sqrt(128.0))
SCALE_L1 = float(1.0 / 32.0)
AWQ_ORDER = [0, 2, 4, 6, 1, 3, 5, 7]
L1FFN_PART = 2
G8 = [list(range(8))]
G4 = [[0, 1, 2, 3], [4, 5, 6, 7]]


class Ctr:
    def __init__(self, stack, nc, name):
        self.h = stack.enter_context(nc.semaphore(name))
        self.n = 0

    def inc(self, inst, k=1):
        inst.then_inc(self.h, k)
        self.n += k
        return inst


class Kb:
    def __init__(self, upto=4, debug=()):
        self.upto = upto
        self.debug = set(debug)
        self.nc = bass.Bass("TRN2", target_bir_lowering=False, debug=False)
        self.st = ExitStack()
        self.live = {}
        self.dbg = {}
        nc = self.nc
        self.sy = nc.sync
        self.sc = nc.scalar
        self.ve = nc.vector
        self.te = nc.tensor
        self.gp = nc.gpsimd
        S = {}
        for name in ["LD", "LDG", "STO", "CV", "EX", "MM", "EV", "CC", "MIS",
                     "DQ", "CONST", "SL0", "SL1", "SL2", "SL3", "RF0", "RF1",
                     "KV0", "KV1", "STK", "STV", "DQS", "DQA", "DQB", "DQC"]:
            S[name] = Ctr(self.st, nc, name)
        self.SL = [S["SL0"], S["SL1"], S["SL2"], S["SL3"]]
        self.RF = [S["RF0"], S["RF1"]]
        self.KV = [S["KV0"], S["KV1"]]
        self.S = S
        self.p_seq = {}
        self.p_ctr = 0
        self._init_arena()

    def _init_arena(self):
        nc = self.nc
        total = nc.sbuf_bytes_remaining - 2048
        self.arena_h = nc.alloc_sbuf_tensor("arena", [P, total],
                                            mybir.dt.uint8)
        base = nc.lookup_mloc(self.arena_h).addr
        self.arena_free = [(base, total)]
        self.sb_regions = {}

    def alloc(self, name, shape, dt):
        size = int(np.prod(shape[1:])) * {F32: 4, BF16: 2, I32: 4}[dt]
        size = (size + 63) // 64 * 64
        for i, (off, sz) in enumerate(self.arena_free):
            if sz >= size:
                self.arena_free[i] = (off + size, sz - size)
                if self.arena_free[i][1] == 0:
                    self.arena_free.pop(i)
                self.sb_regions[name] = (off, size)
                h = self.nc.alloc_sbuf_tensor_at(name, shape, dt, offset=off)
                return h.ap()
        raise MemoryError(f"sbuf arena full allocating {name} ({size}B): "
                          f"free={self.arena_free}")

    def free(self, *names):
        pnames = [n for n in names if n in self.live]
        for n in sorted(pnames, key=lambda n: -self.p_seq[n]):
            self.live.pop(n).__exit__(None, None, None)
            self.p_seq.pop(n)
        for n in names:
            if n in self.sb_regions:
                off, size = self.sb_regions.pop(n)
                self.arena_free.append((off, size))
                self.arena_free.sort()
                merged = []
                for o, s in self.arena_free:
                    if merged and merged[-1][0] + merged[-1][1] == o:
                        merged[-1] = (merged[-1][0], merged[-1][1] + s)
                    else:
                        merged.append((o, s))
                self.arena_free = [tuple(x) for x in merged]

    def palloc(self, name, shape, dt=F32):
        cm = self.nc.psum_tensor(name, shape, dt)
        t = cm.__enter__()
        self.live[name] = cm
        if not hasattr(self, "p_seq"):
            self.p_seq = {}
            self.p_ctr = 0
        self.p_ctr += 1
        self.p_seq[name] = self.p_ctr
        return t

    def din(self, name, shape, dt=F32):
        return self.nc.dram_tensor(name, shape, dt, kind="ExternalInput").ap()

    def dram(self, name, shape, dt):
        return self.nc.dram_tensor(name, shape, dt, kind="Internal").ap()

    def slab_gate(self):
        S = self.S
        for name in ["SL0", "SL1", "SL2", "SL3", "RF0", "RF1", "KV0", "KV1",
                     "LD", "MM", "CV", "EX", "EV", "STO", "STK", "STV", "DQS",
                     "LDG"]:
            self.sy.wait_ge(S[name].h, S[name].n)

    def phase_gate(self):
        S = self.S
        ve = self.ve
        for name in ["LD", "LDG", "STO", "CV", "EX", "MM", "EV", "MIS", "DQ",
                     "CONST", "SL0", "SL1", "SL2", "SL3", "RF0", "RF1",
                     "KV0", "KV1", "STK", "STV", "DQS", "DQA", "DQB", "DQC"]:
            ve.wait_ge(S[name].h, S[name].n)
        S["MIS"].inc(ve.engine_nop())
        g = S["MIS"].n
        for eng in [self.sy, self.sc, self.te, self.gp]:
            eng.wait_ge(S["MIS"].h, g)

    def xsl(self, dc, t0, width):
        h = t0 // 512
        o = t0 % 512
        return self.Xb[h][:, dc * 512 + o: dc * 512 + o + width]

    def build(self):
        self.declare_io()
        self.make_dequant_tasks()
        upto = self.upto
        self.emit_ln(self.R[0], "b0")
        self.prep_transposes()
        self.prep_converts()
        self.emit_l0_attn()
        if upto >= 2:
            self.emit_ln(self.R[1], "b1")
            self.emit_l0_ffn()
        if upto >= 3:
            self.emit_ln(self.R[2], "b2")
            self.emit_l1_attn()
        if upto >= 4:
            self.emit_ln(self.R[3], "b3")
            self.emit_l1_ffn()
        S = self.S
        self.sy.wait_ge(S["STO"].h, S["STO"].n)
        self.sy.wait_ge(S["STK"].h, S["STK"].n)
        self.sy.wait_ge(S["STV"].h, S["STV"].n)
        self.sy.wait_ge(S["DQS"].h, S["DQS"].n)
        self.sy.wait_ge(S["LDG"].h, S["LDG"].n)
        self.sy.wait_ge(S["CC"].h, S["CC"].n)
        self.st.close()
        return self.nc

    def declare_io(self):
        din, dram = self.din, self.dram
        nc = self.nc
        self.x_t = din("x_t", [DM, TL])
        self.w_s = {m: din(m + "_s", [DM, 512]) for m in
                    ["wq", "wk", "wv", "wo", "wb"]}
        self.ffn_bias_pc = din("ffn_bias_pc", [P, DC])
        self.wr = din("wr", [8, DM])
        self.a_flat = din("a_flat", [P, DM])
        self.bm_flat = din("bm_flat", [8 * DM, 16])
        self.wqkv_s = din("wqkv_s", [512, 3072])
        self.wo1_s = din("wo1_s", [P, DM])
        self.bqkv = din("bqkv", [3072])
        self.bqkv_pc = din("bqkv_pc", [P, 24])
        self.bo1_pc = din("bo1_pc", [P, DC])
        self.gq_s = din("gq_s", [512, 1376], I32)
        self.uq_s = din("uq_s", [512, 1376], I32)
        self.dq_s = din("dq_s", [11 * P, 512], I32)
        self.zall = din("zall", [19, 1376], I32)
        self.sall = din("sall", [19, IM])

        self.out_t = nc.dram_tensor("out_t", [DM, TL], F32,
                                    kind="ExternalOutput").ap()
        self.ident = nc.inline_tensor(
            np.eye(P, dtype=ml_dtypes.bfloat16), name="ident").ap()
        self.onescol = nc.inline_tensor(
            np.ones((P, 1), dtype=ml_dtypes.bfloat16), name="onescol").ap()

        self.wT_g = {m: dram(m + "T_g", [8 * P, 4 * 8 * 512], BF16) for m in
                     ["wq", "wk", "wv", "wo", "wb"]}
        self.wT_in = {m: dram(m + "T_in", [P, 4 * 8 * 512], BF16) for m in
                      ["wq", "wk", "wv", "wo", "wb"]}
        self.wqkv_in = dram("wqkv_in", [512, 3072], BF16)
        self.wqkv_g = dram("wqkv_g", [DM, 3072], BF16)
        self.wo1_in = dram("wo1_in", [P, DM], BF16)
        self.wo1_g = dram("wo1_g", [D1, DM], BF16)
        self.gate_in = dram("gate_in", [512, IM], BF16)
        self.gate_g = dram("gate_g", [DM, IM], BF16)
        self.up_in = dram("up_in", [512, IM], BF16)
        self.up_g = dram("up_g", [DM, IM], BF16)
        self.down_in = dram("down_in", [11 * P, DM], BF16)
        self.down_g = dram("down_g", [ICP * P, DM], BF16)
        self.bmT_d = dram("bmT_d", [P, DM], BF16)
        self.zs_d = dram("zs_d", [19, IM], BF16)
        self.sb_d = dram("sb_d", [19, IM], BF16)
        self.arow_d = dram("arow_d", [1, TL], F32)
        self.brow_d = dram("brow_d", [1, TL], F32)
        self.wrt_d = dram("wrt_d", [8, TL], BF16)
        self.l0k_in = dram("l0k_in", [DM, TL], BF16)
        self.l0k_gs = [dram(f"l0k_g{g}", [4 * 8 * P, TL], BF16)
                       for g in range(4)]
        self.l0v_ins = [dram(f"l0v_in{g}", [P, 8 * 8 * 129], BF16)
                        for g in range(4)]
        self.l0v_gs = [dram(f"l0v_g{g}", [4 * P, 8 * 8 * 129], BF16)
                       for g in range(4)]
        self.l1k_in = dram("l1k_in", [D1, TL], BF16)
        self.l1k_g = dram("l1k_g", [4 * D1, TL], BF16)
        self.l1v_in = dram("l1v_in", [TL, D1], BF16)
        self.l1v_g = dram("l1v_g", [4 * TL, D1], BF16)
        R = [self.x_t, dram("R1", [DM, TL], F32), dram("R2", [DM, TL], F32),
             dram("R3", [DM, TL], F32), self.out_t]
        R[self.upto] = self.out_t
        self.R = R

        self.ones_sb = self.alloc("ones_sb", [P, 1], BF16)
        self.id_sb = self.alloc("id_sb", [P, P], BF16)
        self.fc = self.alloc("fc", [P, 4], F32)
        S = self.S
        S["CONST"].inc(self.sy.dma_start(out=self.ones_sb[:], in_=self.onescol), 16)
        S["CONST"].inc(self.sy.dma_start(out=self.id_sb[:], in_=self.ident), 16)
        self.const_ld = S["CONST"].n
        S["CV"].inc(self.ve.memset(self.fc[:, 0:1], EPS))
        S["CV"].inc(self.ve.memset(self.fc[:, 1:2], -C_L0))
        S["CV"].inc(self.ve.memset(self.fc[:, 2:3], -C_L1))
        self.const_cv = S["CV"].n

    def prep_transposes(self):
        sy, sc, ve, te, gp, S = self.sy, self.sc, self.ve, self.te, self.gp, self.S
        wldf = self.alloc("wldf", [P, DC * 512], F32)
        wldb = self.alloc("wldb", [P, DC * 512], BF16)
        tst = self.alloc("tst", [P, 4 * DM], BF16)
        psTr = [self.palloc("psTrA", [P, P], BF16),
                self.palloc("psTrB", [P, P], BF16)]

        cv_done, mm_done, stg_done = [], [], []
        sy.wait_ge(S["CV"].h, self.ln_cv)
        for i, m in enumerate(["wk", "wv", "wq", "wo", "wb"]):
            if i > 0:
                sy.wait_ge(S["CV"].h, cv_done[i - 1])
            S["LD"].inc(sy.dma_start(
                out=wldf[:].rearrange("p (c d) -> p c d", c=DC),
                in_=self.w_s[m].rearrange("(c p) d -> p c d", p=P)), 16)
            ldv = S["LD"].n
            ve.wait_ge(S["LD"].h, ldv)
            if i > 0:
                ve.wait_ge(S["MM"].h, mm_done[i - 1])
            S["CV"].inc(ve.tensor_copy(out=wldb[:], in_=wldf[:]))
            cv_done.append(S["CV"].n)

            te.wait_ge(S["CV"].h, cv_done[i])
            if i == 0:
                te.wait_ge(S["CONST"].h, self.const_ld)
            if i > 0:
                te.wait_ge(S["LDG"].h, stg_done[i - 1])
            for j0 in range(128):
                ec, ds = j0 // 4, j0 % 4
                j = i * 128 + j0
                if j >= 2:
                    te.wait_ge(S["EV"].h, j - 1)
                S["MM"].inc(te.transpose(
                    out=psTr[j % 2][:, :],
                    in_=wldb[:, ec * 512 + ds * P: ec * 512 + (ds + 1) * P],
                    identity=self.id_sb[:]))
                sc.wait_ge(S["MM"].h, S["MM"].n)
                S["EV"].inc(sc.copy(
                    out=tst[:, (ec // 4) * 2048 + ds * 512 + (ec % 4) * P:
                            (ec // 4) * 2048 + ds * 512 + (ec % 4) * P + P],
                    in_=psTr[j % 2][:, :]))
            mm_done.append(S["MM"].n)
            gp.wait_ge(S["EV"].h, S["EV"].n)
            S["LDG"].inc(gp.dma_start(
                out=self.wT_in[m], in_=tst[:]), 16)
            stg_done.append(S["LDG"].n)
            gp.wait_ge(S["LDG"].h, stg_done[i])
            S["CC"].inc(gp.collective_compute(
                "AllGather", ALU.bypass, replica_groups=G8,
                ins=[self.wT_in[m]], outs=[self.wT_g[m]]))
        self.cc_w = {m: i + 1 for i, m in
                     enumerate(["wk", "wv", "wq", "wo", "wb"])}
        self.prep_mm_done = mm_done[-1]
        self.prep_cv = cv_done[-1]
        self.free("psTrA", "psTrB")
        self._prep_bufs = (wldf, wldb)

    def prep_converts(self):
        sy, ve, gp, S = self.sy, self.ve, self.gp, self.S
        wldf, wldb = self._prep_bufs
        sy.wait_ge(S["CV"].h, self.prep_cv)
        S["LD"].inc(sy.dma_start(
            out=wldf[:, 0:4 * 3072].rearrange("p (c d) -> p c d", c=4),
            in_=self.wqkv_s.rearrange("(c p) d -> p c d", p=P)), 16)
        ld1 = S["LD"].n
        ve.wait_ge(S["LD"].h, ld1)
        ve.wait_ge(S["MM"].h, self.prep_mm_done)
        ve.wait_ge(S["LDG"].h, S["LDG"].n)
        S["CV"].inc(ve.tensor_copy(out=wldb[:, 0:4 * 3072],
                                   in_=wldf[:, 0:4 * 3072]))
        cv1 = S["CV"].n
        gp.wait_ge(S["CV"].h, cv1)
        S["LDG"].inc(gp.dma_start(
            out=self.wqkv_in.rearrange("(c p) d -> p c d", p=P),
            in_=wldb[:, 0:4 * 3072].rearrange("p (c d) -> p c d", c=4)), 16)
        gp.wait_ge(S["LDG"].h, S["LDG"].n)
        S["CC"].inc(gp.collective_compute(
            "AllGather", ALU.bypass, replica_groups=G8,
            ins=[self.wqkv_in], outs=[self.wqkv_g]))
        self.cc_wqkv = S["CC"].n
        sy.wait_ge(S["CV"].h, cv1)
        S["LD"].inc(sy.dma_start(out=wldf[:, 0:DM], in_=self.wo1_s), 16)
        ld2 = S["LD"].n
        ve.wait_ge(S["LD"].h, ld2)
        ve.wait_ge(S["LDG"].h, S["LDG"].n)
        S["CV"].inc(ve.tensor_copy(out=wldb[:, 0:DM], in_=wldf[:, 0:DM]))
        cv2 = S["CV"].n
        gp.wait_ge(S["CV"].h, cv2)
        S["LDG"].inc(gp.dma_start(out=self.wo1_in, in_=wldb[:, 0:DM]), 16)
        gp.wait_ge(S["LDG"].h, S["LDG"].n)
        S["CC"].inc(gp.collective_compute(
            "AllGather", ALU.bypass, replica_groups=G8,
            ins=[self.wo1_in], outs=[self.wo1_g]))
        self.cc_wo1 = S["CC"].n
        self.free("wldf", "wldb", "tst")

    def make_dequant_tasks(self):
        tasks = []
        plans = [
            ("g", self.gq_s, self.gate_in, self.gate_g, 4, 344, 0),
            ("u", self.uq_s, self.up_in, self.up_g, 4, 344, 4),
            ("d", self.dq_s, self.down_in, self.down_g, 11, 128, 8),
        ]
        for (tag, src, dst, gout, nch, wq, zb) in plans:
            for k in range(nch):
                for q in range(4):
                    tasks.append((tag, src, dst, gout, nch, wq, zb, k, q))
        self.dq_tasks = tasks
        self.dq_ti = 0
        self._dq_marks = {}
        self._dq_w_sto = {}
        self.cc_gate = self.cc_up = self.cc_down = None

    def emit_dequant_rowprep(self):
        sy, ve, gp, S = self.sy, self.ve, self.gp, self.S
        zl = self.alloc("zl", [19, 1376], I32)
        znq = self.alloc("znq", [19, IM // 4], I32)
        sq = self.alloc("sq", [19, IM // 4], F32)
        sb = self.alloc("sb", [19, IM // 4], BF16)
        zs = self.alloc("zs", [19, IM // 4], BF16)
        S["CONST"].inc(sy.dma_start(out=zl[:], in_=self.zall), 16)
        ve.wait_ge(S["CONST"].h, S["CONST"].n)
        for q4 in range(4):
            cw = IM // 4
            if q4 > 0:
                sy.wait_ge(S["DQ"].h, S["DQ"].n)
            S["RF0"].inc(sy.dma_start(
                out=sq[:], in_=self.sall[:, q4 * cw:(q4 + 1) * cw]), 16)
            ve.wait_ge(S["RF0"].h, S["RF0"].n)
            if q4 > 0:
                ve.wait_ge(S["LDG"].h, S["LDG"].n)
            znr = znq[:].rearrange("x (c j) -> x j c", j=8)
            for j in range(8):
                S["DQ"].inc(ve.tensor_scalar(
                    out=znr[:, AWQ_ORDER[j]:AWQ_ORDER[j] + 1, :],
                    in0=zl[:, q4 * 344:(q4 + 1) * 344],
                    scalar1=4 * j, scalar2=15,
                    op0=ALU.logical_shift_right, op1=ALU.bitwise_and))
            S["DQ"].inc(ve.tensor_copy(out=sb[:], in_=sq[:]))
            ve.wait_ge(S["DQ"].h, S["DQ"].n)
            S["DQ"].inc(ve.tensor_copy(out=zs[:], in_=znq[:]))
            gp.wait_ge(S["DQ"].h, S["DQ"].n)
            S["LDG"].inc(gp.dma_start(
                out=self.sb_d[:, q4 * cw:(q4 + 1) * cw], in_=sb[:]), 16)
            S["LDG"].inc(gp.dma_start(
                out=self.zs_d[:, q4 * cw:(q4 + 1) * cw], in_=zs[:]), 16)
        self.free("zl", "znq", "sq", "sb", "zs")
        self.dq_qraw = self.alloc("dq_qraw", [P, 2 * 344], I32)
        self.dq_nib = self.alloc("dq_nib", [P, 2752], I32)
        self.dq_nibb = self.alloc("dq_nibb", [P, 2752], BF16)
        self.dq_w = self.alloc("dq_w", [P, 2 * 2752], BF16)
        self.dq_zs = self.alloc("dq_zs", [P, 2752], BF16)
        self.dq_sb = self.alloc("dq_sb", [P, 2752], BF16)

    def emit_dequant_task(self):
        if self.dq_ti >= len(self.dq_tasks):
            return False
        (tag, src, dst, gout, nch, wq, zb, k, q) = self.dq_tasks[self.dq_ti]
        ti = self.dq_ti
        self.dq_ti += 1
        ve, gp, sc, S = self.ve, self.gp, self.sc, self.S
        nib = wq * 8
        zrow = zb + k
        dq_ab = S["DQA"] if ti % 2 == 0 else S["DQB"]
        if ti >= 2:
            gp.wait_ge(S["DQ"].h, self._dq_marks[ti - 2])
        dq_ab.inc(gp.dma_start(
            out=self.dq_qraw[:, (ti % 2) * 344:(ti % 2) * 344 + wq],
            in_=src[k * P:(k + 1) * P, q * wq:(q + 1) * wq]), 16)
        if ti >= 1:
            gp.wait_ge(S["DQ"].h, self._dq_marks[ti - 1])
        S["DQC"].inc(gp.dma_start(
            out=self.dq_zs[:, 0:nib],
            in_=self.zs_d[zrow:zrow + 1, q * nib:(q + 1) * nib]
            .partition_broadcast(P)), 16)
        S["DQC"].inc(gp.dma_start(
            out=self.dq_sb[:, 0:nib],
            in_=self.sb_d[zrow:zrow + 1, q * nib:(q + 1) * nib]
            .partition_broadcast(P)), 16)
        gp.wait_ge(dq_ab.h, dq_ab.n)
        gp.wait_ge(S["DQC"].h, S["DQC"].n)
        if ti >= 2:
            gp.wait_ge(S["DQS"].h, S["DQS"].n)
        wsl = self.dq_w[:, (ti % 2) * 2752:(ti % 2) * 2752 + nib]
        nibr = self.dq_nib[:, 0:nib].rearrange("p (c j) -> p j c", j=8)
        ve.wait_ge(dq_ab.h, dq_ab.n)
        if ti >= 1:
            ve.wait_ge(S["DQ"].h, self._dq_marks[ti - 1])
        for j in range(8):
            S["DQ"].inc(ve.tensor_scalar(
                out=nibr[:, AWQ_ORDER[j]:AWQ_ORDER[j] + 1, :],
                in0=self.dq_qraw[:, (ti % 2) * 344:(ti % 2) * 344 + wq],
                scalar1=4 * j, scalar2=15,
                op0=ALU.logical_shift_right, op1=ALU.bitwise_and))
        gp.wait_ge(S["DQ"].h, S["DQ"].n)
        S["DQ"].inc(gp.tensor_copy(out=self.dq_nibb[:, 0:nib],
                                   in_=self.dq_nib[:, 0:nib]))
        gp.wait_ge(S["DQ"].h, S["DQ"].n)
        S["DQ"].inc(gp.tensor_tensor(out=wsl, in0=self.dq_nibb[:, 0:nib],
                                     in1=self.dq_zs[:, 0:nib], op=ALU.subtract))
        gp.wait_ge(S["DQ"].h, S["DQ"].n)
        S["DQ"].inc(gp.tensor_tensor(out=wsl, in0=wsl,
                                     in1=self.dq_sb[:, 0:nib], op=ALU.mult))
        self._dq_marks[ti] = S["DQ"].n
        sc.wait_ge(S["DQ"].h, S["DQ"].n)
        S["DQS"].inc(sc.dma_start(
            out=dst[k * P:(k + 1) * P, q * nib:(q + 1) * nib], in_=wsl), 16)
        self._dq_w_sto[ti] = S["DQS"].n
        if (k == nch - 1) and (q == 3):
            gp.wait_ge(S["DQS"].h, S["DQS"].n)
            S["CC"].inc(gp.collective_compute(
                "AllGather", ALU.bypass, replica_groups=G8,
                ins=[dst], outs=[gout]))
            if tag == "g":
                self.cc_gate = S["CC"].n
            elif tag == "u":
                self.cc_up = S["CC"].n
            else:
                self.cc_down = S["CC"].n
        return True

    def flush_dequant(self):
        while self.emit_dequant_task():
            pass

    def emit_ln(self, rin, tag):
        sy, sc, ve, te, gp, S = self.sy, self.sc, self.ve, self.te, self.gp, self.S
        self.Xb = [self.alloc(f"Xb0{tag}", [P, DC * 512], BF16),
                   self.alloc(f"Xb1{tag}", [P, DC * 512], BF16)]
        self.xb_names = (f"Xb0{tag}", f"Xb1{tag}")
        Rb = self.alloc(f"Rb{tag}", [P, DC * TL], BF16)
        Rf = self.alloc(f"Rf{tag}", [P, 2 * TL], F32)
        Sq = self.alloc(f"Sq{tag}", [P, 2 * TL], BF16)
        ab = self.alloc(f"ab{tag}", [P, 2 * TL], F32)
        rows = self.alloc(f"rows{tag}", [1, 4 * TL], F32)
        psStat = self.palloc(f"psStat{tag}", [1, 4 * 512], F32)

        self.phase_gate()
        cv_marks, sq_marks, mm_marks = [], [], []
        for dc in range(DC):
            rf = self.RF[dc % 2]
            if dc >= 2:
                sy.wait_ge(S["CV"].h, cv_marks[dc - 2])
            rf.inc(sy.dma_start(
                out=Rf[:, (dc % 2) * TL:(dc % 2) * TL + TL],
                in_=rin[dc * P:(dc + 1) * P, :]), 16)
            ve.wait_ge(rf.h, rf.n)
            S["CV"].inc(ve.tensor_copy(
                out=Rb[:, dc * TL:(dc + 1) * TL],
                in_=Rf[:, (dc % 2) * TL:(dc % 2) * TL + TL]))
            cv_marks.append(S["CV"].n)
            sc.wait_ge(S["CV"].h, cv_marks[dc])
            if dc >= 2:
                sc.wait_ge(S["MM"].h, mm_marks[dc - 2])
            S["EX"].inc(sc.activation(
                out=Sq[:, (dc % 2) * TL:(dc % 2) * TL + TL],
                in_=Rb[:, dc * TL:(dc + 1) * TL], func=AF.Square))
            sq_marks.append(S["EX"].n)
            te.wait_ge(S["EX"].h, sq_marks[dc])
            if dc == 0:
                te.wait_ge(S["CONST"].h, self.const_ld)
            for half in range(2):
                S["MM"].inc(te.matmul(
                    out=psStat[:, half * 512:(half + 1) * 512],
                    lhsT=self.ones_sb[:],
                    rhs=Rb[:, dc * TL + half * 512: dc * TL + (half + 1) * 512],
                    start=(dc == 0), stop=(dc == DC - 1)))
                S["MM"].inc(te.matmul(
                    out=psStat[:, 1024 + half * 512:1024 + (half + 1) * 512],
                    lhsT=self.ones_sb[:],
                    rhs=Sq[:, (dc % 2) * TL + half * 512:
                           (dc % 2) * TL + (half + 1) * 512],
                    start=(dc == 0), stop=(dc == DC - 1)))
            mm_marks.append(S["MM"].n)

        mean = rows[:, 0:TL]
        var = rows[:, TL:2 * TL]
        rstd = rows[:, 2 * TL:3 * TL]
        brow = rows[:, 3 * TL:4 * TL]
        ve.wait_ge(S["MM"].h, S["MM"].n)
        S["MIS"].inc(ve.tensor_scalar(out=mean, in0=psStat[:, 0:TL],
                                      scalar1=1.0 / DM, scalar2=None,
                                      op0=ALU.mult))
        ve.wait_ge(S["MIS"].h, S["MIS"].n)
        S["MIS"].inc(ve.scalar_tensor_tensor(
            out=var, in0=mean, scalar=-1.0, in1=mean, op0=ALU.mult,
            op1=ALU.mult))
        ve.wait_ge(S["MIS"].h, S["MIS"].n)
        S["MIS"].inc(ve.scalar_tensor_tensor(
            out=var, in0=psStat[:, TL:2 * TL], scalar=1.0 / DM, in1=var,
            op0=ALU.mult, op1=ALU.add))
        sc.wait_ge(S["MIS"].h, S["MIS"].n)
        sc.wait_ge(S["CV"].h, self.const_cv)
        S["EX"].inc(sc.activation(out=rstd, in_=var, func=AF.Sqrt, bias=self.fc[:1, 0:1]))
        ve.wait_ge(S["EX"].h, S["EX"].n)
        S["MIS"].inc(ve.reciprocal(out=rstd, in_=rstd))
        ve.wait_ge(S["MIS"].h, S["MIS"].n)
        S["MIS"].inc(ve.scalar_tensor_tensor(
            out=brow, in0=mean, scalar=-1.0, in1=rstd, op0=ALU.mult,
            op1=ALU.mult))
        gp.wait_ge(S["MIS"].h, S["MIS"].n)
        S["LDG"].inc(gp.dma_start(out=self.arow_d, in_=rstd), 16)
        S["LDG"].inc(gp.dma_start(out=self.brow_d, in_=brow), 16)
        sy.wait_ge(S["LDG"].h, S["LDG"].n)
        S["LD"].inc(sy.dma_start(
            out=ab[:, 0:TL], in_=self.arow_d.partition_broadcast(P)), 16)
        S["LD"].inc(sy.dma_start(
            out=ab[:, TL:2 * TL], in_=self.brow_d.partition_broadcast(P)), 16)
        abld = S["LD"].n
        ve.wait_ge(S["LD"].h, abld)
        for dc in range(DC):
            for h in range(2):
                S["CV"].inc(ve.tensor_tensor(
                    out=self.Xb[h][:, dc * 512:(dc + 1) * 512],
                    in0=Rb[:, dc * TL + h * 512:dc * TL + (h + 1) * 512],
                    in1=ab[:, h * 512:h * 512 + 512], op=ALU.mult))
                ve.wait_ge(S["CV"].h, S["CV"].n)
                S["CV"].inc(ve.tensor_tensor(
                    out=self.Xb[h][:, dc * 512:(dc + 1) * 512],
                    in0=self.Xb[h][:, dc * 512:(dc + 1) * 512],
                    in1=ab[:, TL + h * 512:TL + h * 512 + 512], op=ALU.add))
                ve.wait_ge(S["CV"].h, S["CV"].n)
        self.ln_cv = S["CV"].n
        self.free(f"Rb{tag}", f"Rf{tag}", f"Sq{tag}", f"ab{tag}",
                  f"rows{tag}", f"psStat{tag}")

    def proj_fm(self, wsrc, eg_list, evac_fn, col0=0, cc_wait=None,
                extra_mm=None, tag="", kdc=DC, tiled=False):
        sy, te, S = self.sy, self.te, self.S
        if tiled:
            nslab = 4
            wsl = self.alloc(f"wsl{tag}", [P, nslab * 2048], BF16)
            self.slab_gate()
            slab_marks = {}
            si = 0
            ev_mark = None
            for gi, eg in enumerate(eg_list):
                for r in range(8):
                    sl = self.SL[si % nslab]
                    if si >= nslab:
                        sy.wait_ge(S["MM"].h, slab_marks[si - nslab])
                    if si == 0 and cc_wait is not None:
                        sy.wait_ge(S["CC"].h, cc_wait)
                    sl.inc(sy.dma_start(
                        out=wsl[:, (si % nslab) * 2048:
                                (si % nslab) * 2048 + 2048],
                        in_=wsrc[r * P:(r + 1) * P,
                                 eg * 2048:(eg + 1) * 2048]), 16)
                    te.wait_ge(sl.h, sl.n)
                    if r == 0:
                        te.wait_ge(S["CV"].h, self.ln_cv)
                        if ev_mark is not None:
                            te.wait_ge(S["EV"].h, ev_mark)
                    for dcl in range(4):
                        dc = r * 4 + dcl
                        for sub in range(4):
                            for th in range(2):
                                S["MM"].inc(te.matmul(
                                    out=self.ps8[:, sub * TL + th * 512:
                                                 sub * TL + (th + 1) * 512],
                                    lhsT=wsl[:, (si % nslab) * 2048 +
                                             dcl * 512 + sub * P:
                                             (si % nslab) * 2048 +
                                             dcl * 512 + (sub + 1) * P],
                                    rhs=self.xsl(dc, th * 512, 512),
                                    start=(dc == 0),
                                    stop=(dc == DC - 1 and extra_mm is None)))
                    slab_marks[si] = S["MM"].n
                    si += 1
                if extra_mm is not None:
                    extra_mm(eg)
                evac_fn(gi, eg, S["MM"].n)
                ev_mark = S["EV"].n
            self.free(f"wsl{tag}")
            return
        nslab = 4
        wsl = self.alloc(f"wsl{tag}", [P, nslab * 512], BF16)
        self.slab_gate()
        slab_marks = {}
        si = 0
        ev_mark = None
        for gi, eg in enumerate(eg_list):
            for dc in range(kdc):
                sl = self.SL[si % nslab]
                if si >= nslab:
                    sy.wait_ge(S["MM"].h, slab_marks[si - nslab])
                if si == 0 and cc_wait is not None:
                    sy.wait_ge(S["CC"].h, cc_wait)
                sl.inc(sy.dma_start(
                    out=wsl[:, (si % nslab) * 512:(si % nslab) * 512 + 512],
                    in_=wsrc[dc * P:(dc + 1) * P,
                             col0 + eg * 512:col0 + (eg + 1) * 512]), 16)
                te.wait_ge(sl.h, sl.n)
                if dc == 0:
                    te.wait_ge(S["CV"].h, self.ln_cv)
                    if ev_mark is not None:
                        te.wait_ge(S["EV"].h, ev_mark)
                for sub in range(4):
                    for th in range(2):
                        S["MM"].inc(te.matmul(
                            out=self.ps8[:, sub * TL + th * 512:
                                         sub * TL + (th + 1) * 512],
                            lhsT=wsl[:, (si % nslab) * 512 + sub * P:
                                     (si % nslab) * 512 + (sub + 1) * P],
                            rhs=self.xsl(dc, th * 512, 512),
                            start=(dc == 0),
                            stop=(dc == kdc - 1 and extra_mm is None)))
                slab_marks[si] = S["MM"].n
                si += 1
            if extra_mm is not None:
                extra_mm(eg)
            evac_fn(gi, eg, S["MM"].n)
            ev_mark = S["EV"].n
        self.free(f"wsl{tag}")

    def resid_evac(self, rprev, rnext, state):
        sy, sc, ve, S = self.sy, self.sc, self.ve, self.S
        rst_marks = state.setdefault("rst_marks", {})
        rin_cv = state.setdefault("rin_cv", {})

        def evac(gi, eg, mmv):
            rf = self.RF[gi % 2]
            for sub in range(4):
                oc = eg * 4 + sub
                if oc >= 4:
                    sy.wait_ge(S["CV"].h, rin_cv[oc - 4])
                rf.inc(sy.dma_start(
                    out=state["Rin"][:, (oc % 4) * TL:(oc % 4) * TL + TL],
                    in_=rprev[oc * P:(oc + 1) * P, :]), 16)
            rin_ld = (rf, rf.n)
            for sub in range(4):
                oc = eg * 4 + sub
                ve.wait_ge(S["MM"].h, mmv)
                ve.wait_ge(rin_ld[0].h, rin_ld[1])
                if oc >= 2:
                    ve.wait_ge(S["STO"].h, S["STO"].n)
                if state.get("bias") is not None:
                    ve.wait_ge(S["CONST"].h, S["CONST"].n)
                    S["CV"].inc(ve.scalar_tensor_tensor(
                        out=state["Rst"][:, (oc % 2) * TL:(oc % 2) * TL + TL],
                        in0=self.ps8[:, sub * TL:(sub + 1) * TL],
                        scalar=state["bias"][:, oc:oc + 1],
                        in1=state["Rin"][:, (oc % 4) * TL:(oc % 4) * TL + TL],
                        op0=ALU.add, op1=ALU.add))
                else:
                    S["CV"].inc(ve.tensor_tensor(
                        out=state["Rst"][:, (oc % 2) * TL:(oc % 2) * TL + TL],
                        in0=self.ps8[:, sub * TL:(sub + 1) * TL],
                        in1=state["Rin"][:, (oc % 4) * TL:(oc % 4) * TL + TL],
                        op=ALU.add))
                rin_cv[oc] = S["CV"].n
                sc.wait_ge(S["CV"].h, S["CV"].n)
                S["STO"].inc(sc.dma_start(
                    out=rnext[oc * P:(oc + 1) * P, :],
                    in_=state["Rst"][:, (oc % 2) * TL:(oc % 2) * TL + TL]), 16)
                rst_marks[oc] = S["STO"].n
                S["EV"].inc(ve.engine_nop())
        return evac

    def emit_l0_attn(self):
        sy, sc, ve, te, gp, S = self.sy, self.sc, self.ve, self.te, self.gp, self.S
        self.ps8 = self.palloc("ps8a", [P, 4 * TL], F32)
        Qb = self.alloc("Qb", [P, NH * TL], BF16)
        kst = self.alloc("kst", [P, 4 * TL], BF16)

        kst_sto = {}

        def evac_k(gi, eg, mmv):
            sc.wait_ge(S["MM"].h, mmv)
            ve.wait_ge(S["MM"].h, mmv)
            sc.wait_ge(S["STK"].h, S["STK"].n)
            ve.wait_ge(S["STK"].h, S["STK"].n)
            for sub in range(4):
                if sub < 2:
                    S["EV"].inc(sc.copy(out=kst[:, sub * TL:(sub + 1) * TL],
                                        in_=self.ps8[:, sub * TL:(sub + 1) * TL]))
                else:
                    S["EV"].inc(ve.tensor_copy(
                        out=kst[:, sub * TL:(sub + 1) * TL],
                        in_=self.ps8[:, sub * TL:(sub + 1) * TL]))
            sc.wait_ge(S["EV"].h, S["EV"].n)
            for sub in range(4):
                ec = eg * 4 + sub
                S["STK"].inc(sc.dma_start(
                    out=self.l0k_in[ec * P:(ec + 1) * P, :],
                    in_=kst[:, sub * TL:(sub + 1) * TL]), 16)
            kst_sto[gi] = S["STK"].n

        self.cc_l0k = {}

        def evac_k_cc(gi, eg, mmv):
            evac_k(gi, eg, mmv)
            if gi % 2 == 1:
                hg = gi // 2
                gp.wait_ge(S["STK"].h, S["STK"].n)
                S["CC"].inc(gp.collective_compute(
                    "AllGather", ALU.bypass, replica_groups=G4,
                    ins=[self.l0k_in[hg * 8 * P:(hg + 1) * 8 * P, :]],
                    outs=[self.l0k_gs[hg]]))
                self.cc_l0k[hg] = S["CC"].n

        self.proj_fm(self.wT_g["wk"], list(range(8)), evac_k_cc,
                     cc_wait=self.cc_w["wk"], tag="k0", tiled=True)

        vst = self.alloc("vst", [P, 2 * 516], BF16)
        vsl = self.alloc("vslab", [P, 4 * 2048], BF16)
        self.slab_gate()
        ve.wait_ge(S["MM"].h, S["MM"].n)
        ve.wait_ge(S["STV"].h, S["STV"].n)
        for s in range(2):
            for sub in range(4):
                S["CV"].inc(ve.memset(
                    vst[:, s * 516 + sub * 129 + P: s * 516 + sub * 129 + P + 1],
                    1.0))
        vst_ones = S["CV"].n
        self.cc_l0v = {}
        vs_marks = {}
        sto_marks = {}
        si = 0
        ev_mark = None
        for eg in range(8):
            for r in range(8):
                sl = self.SL[si % 4]
                if si >= 4:
                    sy.wait_ge(S["MM"].h, vs_marks[si - 4])
                if si == 0:
                    sy.wait_ge(S["CC"].h, self.cc_w["wv"])
                sl.inc(sy.dma_start(
                    out=vsl[:, (si % 4) * 2048:(si % 4) * 2048 + 2048],
                    in_=self.wT_g["wv"][r * P:(r + 1) * P,
                                        eg * 2048:(eg + 1) * 2048]), 16)
                te.wait_ge(sl.h, sl.n)
                if r == 0:
                    te.wait_ge(S["CV"].h, self.ln_cv)
                    if ev_mark is not None:
                        te.wait_ge(S["EV"].h, ev_mark)
                for dcl in range(4):
                    dc = r * 4 + dcl
                    for tc in range(8):
                        S["MM"].inc(te.matmul(
                            out=self.ps8[:, tc * 512:(tc + 1) * 512],
                            lhsT=self.xsl(dc, tc * P, P),
                            rhs=vsl[:, (si % 4) * 2048 + dcl * 512:
                                    (si % 4) * 2048 + (dcl + 1) * 512],
                            start=(dc == 0), stop=(dc == DC - 1)))
                vs_marks[si] = S["MM"].n
                si += 1
            mmv = S["MM"].n
            for tc in range(8):
                ve.wait_ge(S["MM"].h, mmv)
                u = eg * 8 + tc
                if u == 0:
                    ve.wait_ge(S["CV"].h, vst_ones)
                if u >= 2:
                    ve.wait_ge(S["STV"].h, S["STV"].n)
                S["EV"].inc(ve.tensor_copy(
                    out=vst[:, (tc % 2) * 516:(tc % 2) * 516 + 516]
                    .rearrange("p (s c) -> p s c", c=129)[:, :, 0:P],
                    in_=self.ps8[:, tc * 512:(tc + 1) * 512]
                    .rearrange("p (s c) -> p s c", c=P)))
                sc.wait_ge(S["EV"].h, S["EV"].n)
                for sub in range(4):
                    h = eg * 4 + sub
                    hg, hl = h // 8, h % 8
                    S["STV"].inc(sc.dma_start(
                        out=self.l0v_ins[hg][:, (hl * 8 + tc) * 129:
                                             (hl * 8 + tc + 1) * 129],
                        in_=vst[:, (tc % 2) * 516 + sub * 129:
                                (tc % 2) * 516 + (sub + 1) * 129]), 16)
                sto_marks[u] = S["STV"].n
            ev_mark = S["EV"].n
            if eg % 2 == 1:
                hg = eg // 2
                gp.wait_ge(S["STV"].h, S["STV"].n)
                S["CC"].inc(gp.collective_compute(
                    "AllGather", ALU.bypass, replica_groups=G4,
                    ins=[self.l0v_ins[hg]], outs=[self.l0v_gs[hg]]))
                self.cc_l0v[hg] = S["CC"].n
        self.free("vslab", "vst")

        def evac_q(gi, eg, mmv):
            sc.wait_ge(S["MM"].h, mmv)
            ve.wait_ge(S["MM"].h, mmv)
            for sub in range(4):
                ec = eg * 4 + sub
                eng = sc if sub < 2 else ve
                S["EV"].inc(eng.tensor_copy(
                    out=Qb[:, ec * TL:(ec + 1) * TL],
                    in_=self.ps8[:, sub * TL:(sub + 1) * TL]) if eng is ve
                    else eng.copy(out=Qb[:, ec * TL:(ec + 1) * TL],
                                  in_=self.ps8[:, sub * TL:(sub + 1) * TL]))

        self.proj_fm(self.wT_g["wq"], list(range(8)), evac_q,
                     cc_wait=self.cc_w["wq"], tag="q0", tiled=True)
        q_mm = S["MM"].n
        q_ev = S["EV"].n
        self.free("kst", "ps8a")

        self.phase_gate()
        self.emit_dequant_rowprep()

        self.free(*self.xb_names)
        self.phase_gate()
        ATok = self.alloc("ATok", [P, NH * TL], BF16)
        Kh = self.alloc("Kh", [P, 2 * 4096], BF16)
        Vh = self.alloc("Vh", [P, 2 * KB * 129], BF16)
        Pt = self.alloc("Pt", [P, 2 * 1024], BF16)
        zin = self.alloc("zin", [P, 2], F32)
        psS = [self.palloc("psS0", [P, 1024], F32),
               self.palloc("psS1", [P, 1024], F32)]
        psO = [self.palloc(f"psO{i}", [P, 129], F32) for i in range(4)]
        NPAIR = KB // 2

        ve.wait_ge(S["MM"].h, q_mm)
        ve.wait_ge(S["EV"].h, q_ev)
        S["CV"].inc(ve.engine_nop())
        vh_cv = S["CV"].n

        kh_ld = {}
        exp_marks = {}
        pv_marks = {}
        norm_marks = {}
        tr_marks = {}
        self._pending_he = None

        for h in range(NH):
            hg = h // 8
            hl = h % 8
            sy.wait_ge(S["CC"].h, max(self.cc_l0k[hg], self.cc_l0v[hg]))
            if h >= 2:
                sy.wait_ge(S["MM"].h, pv_marks[(h - 2, 1, NPAIR - 1)])
            else:
                sy.wait_ge(S["CV"].h, vh_cv)
            kv = self.KV[h % 2]
            for r in range(4):
                kv.inc(sy.dma_start(
                    out=Kh[:, (h % 2) * 4096 + r * TL:
                           (h % 2) * 4096 + (r + 1) * TL],
                    in_=self.l0k_gs[hg][r * 8 * P + hl * P:
                                        r * 8 * P + (hl + 1) * P, :]), 16)
                kv.inc(sy.dma_start(
                    out=Vh[:, (h % 2) * KB * 129 + r * 8 * 129:
                           (h % 2) * KB * 129 + (r + 1) * 8 * 129],
                    in_=self.l0v_gs[hg][r * P:(r + 1) * P,
                                        hl * 8 * 129:(hl + 1) * 8 * 129]), 16)
            kh_ld[h] = (kv, kv.n)

            for qh in range(2):
                hi = h * 2 + qh
                s_marks = {}

                def emit_S(pp):
                    gidx = hi * NPAIR + pp
                    te.wait_ge(kh_ld[h][0].h, kh_ld[h][1])
                    if gidx >= 2:
                        pg = gidx - 2
                        pkey = (pg // (2 * NPAIR), (pg // NPAIR) % 2,
                                pg % NPAIR)
                        te.wait_ge(S["EX"].h, exp_marks[pkey])
                    for half in range(2):
                        kb = 2 * pp + half
                        S["MM"].inc(te.matmul(
                            out=psS[pp % 2][:, half * 512:(half + 1) * 512],
                            lhsT=Kh[:, (h % 2) * 4096 + kb * P:
                                    (h % 2) * 4096 + (kb + 1) * P],
                            rhs=Qb[:, h * TL + qh * 512:
                                   h * TL + (qh + 1) * 512],
                            start=True, stop=True))
                    s_marks[pp] = S["MM"].n

                def emit_exp_pv(pp):
                    gidx = hi * NPAIR + pp
                    sc.wait_ge(S["MM"].h, s_marks[pp])
                    if gidx >= 2:
                        pg = gidx - 2
                        pkey = (pg // (2 * NPAIR), (pg // NPAIR) % 2,
                                pg % NPAIR)
                        sc.wait_ge(S["MM"].h, pv_marks[pkey])
                    S["EX"].inc(sc.activation(
                        out=Pt[:, (pp % 2) * 1024:(pp % 2) * 1024 + 1024],
                        in_=psS[pp % 2][:, :],
                        func=AF.Exp, scale=SCALE_L0, bias=self.fc[:, 1:2]))
                    exp_marks[(h, qh, pp)] = S["EX"].n
                    te.wait_ge(S["EX"].h, exp_marks[(h, qh, pp)])
                    if pp == 0 and hi > 0:
                        pk = (h, 0) if qh == 1 else (h - 1, 1)
                        te.wait_ge(S["CV"].h, norm_marks[pk])
                        te.wait_ge(S["EV"].h, tr_marks[pk])
                    for half in range(2):
                        kb = 2 * pp + half
                        for qs in range(4):
                            S["MM"].inc(te.matmul(
                                out=psO[qs][:, :],
                                lhsT=Pt[:, (pp % 2) * 1024 + half * 512 +
                                        qs * P:
                                        (pp % 2) * 1024 + half * 512 +
                                        (qs + 1) * P],
                                rhs=Vh[:, (h % 2) * KB * 129 + kb * 129:
                                       (h % 2) * KB * 129 + (kb + 1) * 129],
                                start=(kb == 0), stop=(kb == KB - 1)))
                    pv_marks[(h, qh, pp)] = S["MM"].n

                emit_S(0)
                if self._pending_he is not None:
                    self._pending_he()
                    self._pending_he = None
                    if qh == 0:
                        self.emit_dequant_task()
                emit_S(1)
                for pp in range(2, NPAIR):
                    emit_exp_pv(pp - 2)
                    emit_S(pp)
                emit_exp_pv(NPAIR - 2)
                emit_exp_pv(NPAIR - 1)

                def headend(h=h, qh=qh):
                    pvv = pv_marks[(h, qh, NPAIR - 1)]
                    for qs in range(4):
                        ve.wait_ge(S["MM"].h, pvv)
                        S["MIS"].inc(ve.reciprocal(
                            out=zin[:, qs % 2:qs % 2 + 1],
                            in_=psO[qs][:, P:P + 1]))
                        ve.wait_ge(S["MIS"].h, S["MIS"].n)
                        S["CV"].inc(ve.tensor_scalar(
                            out=ATok[:, h * TL + qh * 512 + qs * P:
                                     h * TL + qh * 512 + (qs + 1) * P],
                            in0=psO[qs][:, 0:P],
                            scalar1=zin[:, qs % 2:qs % 2 + 1], scalar2=None,
                            op0=ALU.mult))
                        ve.wait_ge(S["CV"].h, S["CV"].n)
                    norm_marks[(h, qh)] = S["CV"].n
                    tr_marks[(h, qh)] = S["EV"].n

                self._pending_he = headend
        self._pending_he()
        self._pending_he = None

        self.free("Kh", "Vh", "Pt", "zin", "psS0", "psS1",
                  "psO0", "psO1", "psO2", "psO3", "Qb")
        AT = self.alloc("AT", [P, NH * TL], BF16)
        psT2 = [self.palloc("psT2a", [P, P], BF16),
                self.palloc("psT2b", [P, P], BF16)]
        te.wait_ge(S["CV"].h, S["CV"].n)
        te.wait_ge(S["MM"].h, S["MM"].n)
        tr_ev = {}
        for i in range(NH * 8):
            hh, qq = i // 8, i % 8
            if i >= 2:
                te.wait_ge(*tr_ev[i - 2])
            S["MM"].inc(te.transpose(
                out=psT2[i % 2][:, :],
                in_=ATok[:, hh * TL + qq * P: hh * TL + (qq + 1) * P],
                identity=self.id_sb[:]))
            if i % 2 == 0:
                sc.wait_ge(S["MM"].h, S["MM"].n)
                S["EV"].inc(sc.copy(
                    out=AT[:, hh * TL + qq * P: hh * TL + (qq + 1) * P],
                    in_=psT2[i % 2][:, :]))
                tr_ev[i] = (S["EV"].h, S["EV"].n)
            else:
                ve.wait_ge(S["MM"].h, S["MM"].n)
                S["MIS"].inc(ve.tensor_copy(
                    out=AT[:, hh * TL + qq * P: hh * TL + (qq + 1) * P],
                    in_=psT2[i % 2][:, :]))
                tr_ev[i] = (S["MIS"].h, S["MIS"].n)
        te.wait_ge(S["EV"].h, S["EV"].n)
        te.wait_ge(S["MIS"].h, S["MIS"].n)
        S["MM"].inc(te.nop())
        at_ev_mm = S["MM"].n
        at_ev = S["EV"].n

        self.free("ATok", "psT2a", "psT2b")

        self.phase_gate()
        self.ps8 = self.palloc("ps8o", [P, 4 * TL], F32)
        state = {"Rin": self.alloc("RinA", [P, 4 * TL], F32),
                 "Rst": self.alloc("RstA", [P, 2 * TL], F32),
                 "bias": None}
        ev_fn = self.resid_evac(self.R[0], self.R[1], state)

        orig_xsl = self.xsl
        self.xsl = lambda dc, t0, width: AT[:, dc * TL + t0: dc * TL + t0 + width]
        self.ln_cv = self.S["CV"].n
        te.wait_ge(S["MM"].h, at_ev_mm)

        def ev_dq(gi, eg, mmv):
            ev_fn(gi, eg, mmv)
            for _ in range(3):
                self.emit_dequant_task()

        self.proj_fm(self.wT_g["wo"], list(range(8)), ev_dq,
                     cc_wait=self.cc_w["wo"], tag="o0", tiled=True)
        self.xsl = orig_xsl
        self.free("RinA", "RstA", "AT", "ps8o")

    def emit_l0_ffn(self):
        sy, sc, ve, te, gp, S = self.sy, self.sc, self.ve, self.te, self.gp, self.S
        self.phase_gate()

        aux = self.alloc("aux", [P, 3 * DM], F32)
        auxb = self.alloc("auxb", [P, 3 * DM], BF16)
        wrt_sb = self.alloc("wrt_sb", [P, DC * 8], BF16)
        at_a = self.alloc("at_a", [P, DC * P], BF16)
        bst = self.alloc("bst", [16, DM], BF16)
        psTr = [self.palloc("psTra", [P, P], BF16),
                self.palloc("psTrb", [P, P], BF16)]
        S["LD"].inc(sy.dma_start(out=aux[0:8, 0:DM], in_=self.wr), 16)
        S["LD"].inc(sy.dma_start(out=aux[:, DM:2 * DM], in_=self.a_flat), 16)
        S["LD"].inc(sy.dma_start(
            out=aux[:, 2 * DM:3 * DM].rearrange("p (g r) -> p g r", r=16),
            in_=self.bm_flat.rearrange("(g p) r -> p g r", p=P)), 16)
        ld0 = S["LD"].n
        ve.wait_ge(S["LD"].h, ld0)
        S["CV"].inc(ve.tensor_copy(out=auxb[0:8, 0:DM], in_=aux[0:8, 0:DM]))
        S["CV"].inc(ve.tensor_copy(out=auxb[:, DM:2 * DM], in_=aux[:, DM:2 * DM]))
        S["CV"].inc(ve.tensor_copy(out=auxb[:, 2 * DM:3 * DM],
                                   in_=aux[:, 2 * DM:3 * DM]))
        cv0 = S["CV"].n
        te.wait_ge(S["CV"].h, cv0)
        tev = {}
        ti = 0

        def tr_step(in_ap, out_cb, mp=P):
            nonlocal ti
            if ti >= 2:
                te.wait_ge(S["EV"].h, tev[ti - 2])
            S["MM"].inc(te.transpose(out=psTr[ti % 2][0:mp, 0:P], in_=in_ap,
                                     identity=self.id_sb[:]))
            sc.wait_ge(S["MM"].h, S["MM"].n)
            S["EV"].inc(out_cb(psTr[ti % 2]))
            tev[ti] = S["EV"].n
            ti += 1

        for dc in range(DC):
            nonloc = dc
            if ti >= 2:
                te.wait_ge(S["EV"].h, tev[ti - 2])
            S["MM"].inc(te.transpose(out=psTr[ti % 2][0:P, 0:8],
                                     in_=auxb[0:8, dc * P:(dc + 1) * P],
                                     identity=self.id_sb[0:8, 0:8]))
            sc.wait_ge(S["MM"].h, S["MM"].n)
            S["EV"].inc(sc.copy(out=wrt_sb[:, dc * 8:(dc + 1) * 8],
                                in_=psTr[ti % 2][0:P, 0:8]))
            tev[ti] = S["EV"].n
            ti += 1
        for dc in range(DC):
            tr_step(auxb[:, DM + dc * P:DM + (dc + 1) * P],
                    lambda ps, dc=dc: sc.copy(
                        out=at_a[:, dc * P:(dc + 1) * P], in_=ps[:, :]))
        bst_sto = {}
        for e in range(8):
            for oc in range(DC):
                g = e * DC + oc
                if oc == 0 and e >= 1:
                    sc.wait_ge(S["STO"].h, S["STO"].n)
                tr_step(auxb[:, 2 * DM + g * 16:2 * DM + (g + 1) * 16],
                        lambda ps, oc=oc: sc.copy(
                            out=bst[0:16, oc * P:(oc + 1) * P],
                            in_=ps[0:16, 0:P]), mp=16)
            sc.wait_ge(S["EV"].h, S["EV"].n)
            S["STO"].inc(sc.dma_start(
                out=self.bmT_d[e * 16:(e + 1) * 16, :], in_=bst[0:16, :]), 16)
            bst_sto[e] = S["STO"].n
        self.free("aux", "auxb", "bst", "psTra", "psTrb")
        bmT_sb = self.alloc("bmT_sb", [P, DM], BF16)
        sy.wait_ge(S["STO"].h, S["STO"].n)
        S["CONST"].inc(sy.dma_start(out=bmT_sb[:], in_=self.bmT_d), 16)
        bm_ld = S["CONST"].n

        psL = self.palloc("psL", [P, 8], F32)
        psW2 = [self.palloc("psW2a", [P, P], BF16),
                self.palloc("psW2b", [P, P], BF16)]
        rr = self.alloc("rr", [P, 64], F32)
        wqb = self.alloc("wqb", [P, 2 * 8], BF16)
        wt_sb = self.alloc("wt_sb", [8, TL], BF16)
        wbc = self.alloc("wbc", [P, TL], BF16)
        hwT = self.alloc("hwT", [P, TL], BF16)
        wt_ev = {}
        wt_mm = {}
        rt_mis = 0
        te.wait_ge(S["EV"].h, S["EV"].n)
        for tc in range(8):
            te.wait_ge(S["CV"].h, self.ln_cv)
            if tc > 0:
                te.wait_ge(S["MIS"].h, rt_mis)
            for dc in range(DC):
                S["MM"].inc(te.matmul(
                    out=psL[:, :], lhsT=self.xsl(dc, tc * P, P),
                    rhs=wrt_sb[:, dc * 8:(dc + 1) * 8],
                    start=(dc == 0), stop=(dc == DC - 1)))
            mml = S["MM"].n
            lg = rr[:, 0:8]
            m1 = rr[:, 8:9]
            nm1 = rr[:, 9:10]
            dd = rr[:, 16:24]
            ind = rr[:, 24:32]
            msk = rr[:, 32:40]
            m2 = rr[:, 10:11]
            e1 = rr[:, 40:48]
            ed = rr[:, 11:12]
            r2 = rr[:, 12:13]
            ind2 = rr[:, 48:56]
            w1 = rr[:, 56:64]
            ve.wait_ge(S["MM"].h, mml)
            S["MIS"].inc(ve.tensor_copy(out=lg, in_=psL[:, :]))
            ve.wait_ge(S["MIS"].h, S["MIS"].n)
            S["MIS"].inc(ve.reduce_max(out=m1, in_=lg, axis=AXX))
            ve.wait_ge(S["MIS"].h, S["MIS"].n)
            S["MIS"].inc(ve.tensor_scalar(out=nm1, in0=m1, scalar1=-1.0,
                                          scalar2=None, op0=ALU.mult))
            ve.wait_ge(S["MIS"].h, S["MIS"].n)
            S["MIS"].inc(ve.tensor_scalar(out=dd, in0=lg, scalar1=nm1,
                                          scalar2=None, op0=ALU.add))
            ve.wait_ge(S["MIS"].h, S["MIS"].n)
            S["MIS"].inc(ve.tensor_single_scalar(out=ind, in_=dd, scalar=0.0,
                                                 op=ALU.is_ge))
            ve.wait_ge(S["MIS"].h, S["MIS"].n)
            S["MIS"].inc(ve.scalar_tensor_tensor(
                out=msk, in0=ind, scalar=-1e9, in1=lg, op0=ALU.mult,
                op1=ALU.add))
            ve.wait_ge(S["MIS"].h, S["MIS"].n)
            S["MIS"].inc(ve.reduce_max(out=m2, in_=msk, axis=AXX))
            ve.wait_ge(S["MIS"].h, S["MIS"].n)
            sc.wait_ge(S["MIS"].h, S["MIS"].n)
            S["EX"].inc(sc.activation(out=e1, in_=lg, func=AF.Exp, bias=nm1))
            S["EX"].inc(sc.activation(out=ed, in_=m2, func=AF.Exp, bias=nm1))
            ve.wait_ge(S["EX"].h, S["EX"].n)
            S["MIS"].inc(ve.tensor_single_scalar(out=r2, in_=ed, scalar=1.0,
                                                 op=ALU.add))
            ve.wait_ge(S["MIS"].h, S["MIS"].n)
            S["MIS"].inc(ve.reciprocal(out=r2, in_=r2))
            ve.wait_ge(S["MIS"].h, S["MIS"].n)
            S["MIS"].inc(ve.tensor_single_scalar(out=r2, in_=r2, scalar=2.0,
                                                 op=ALU.mult))
            S["MIS"].inc(ve.tensor_scalar(out=ind2, in0=lg, scalar1=m2,
                                          scalar2=None, op0=ALU.is_ge))
            ve.wait_ge(S["MIS"].h, S["MIS"].n)
            S["MIS"].inc(ve.tensor_tensor(out=w1, in0=e1, in1=ind2,
                                          op=ALU.mult))
            ve.wait_ge(S["MIS"].h, S["MIS"].n)
            if tc >= 2:
                ve.wait_ge(S["MM"].h, wt_mm[tc - 2])
            S["MIS"].inc(ve.tensor_scalar(
                out=wqb[:, (tc % 2) * 8:(tc % 2) * 8 + 8], in0=w1, scalar1=r2,
                scalar2=None, op0=ALU.mult))
            rt_mis = S["MIS"].n
            te.wait_ge(S["MIS"].h, rt_mis)
            if tc >= 2:
                te.wait_ge(S["EV"].h, wt_ev[tc - 2])
            S["MM"].inc(te.transpose(
                out=psW2[tc % 2][0:8, 0:P],
                in_=wqb[:, (tc % 2) * 8:(tc % 2) * 8 + 8],
                identity=self.id_sb[:]))
            wt_mm[tc] = S["MM"].n
            sc.wait_ge(S["MM"].h, S["MM"].n)
            S["EV"].inc(sc.copy(out=wt_sb[0:8, tc * P:(tc + 1) * P],
                                in_=psW2[tc % 2][0:8, 0:P]))
            wt_ev[tc] = S["EV"].n
        gp.wait_ge(S["EV"].h, S["EV"].n)
        S["LDG"].inc(gp.dma_start(out=self.wrt_d, in_=wt_sb[0:8, :]), 16)
        sy.wait_ge(S["LDG"].h, S["LDG"].n)
        for e in range(8):
            S["RF1"].inc(sy.dma_start(
                out=wbc[e * 16:(e + 1) * 16, :],
                in_=self.wrt_d[e:e + 1, :].partition_broadcast(16)), 16)
        wbc_ld = ("RF1", S["RF1"].n)

        psH = self.palloc("psH", [P, TL], F32)
        te.wait_ge(S["CONST"].h, bm_ld)
        for dc in range(DC):
            for th in range(2):
                S["MM"].inc(te.matmul(
                    out=psH[:, th * 512:(th + 1) * 512],
                    lhsT=at_a[:, dc * P:(dc + 1) * P],
                    rhs=self.xsl(dc, th * 512, 512),
                    start=(dc == 0), stop=(dc == DC - 1)))
        h_mm = S["MM"].n
        ve.wait_ge(S["MM"].h, h_mm)
        ve.wait_ge(S[wbc_ld[0]].h, wbc_ld[1])
        S["CV"].inc(ve.tensor_tensor(out=hwT[:], in0=psH[:, :], in1=wbc[:],
                                     op=ALU.mult))
        hw_cv = S["CV"].n
        self.free("rr", "wqb", "psL", "psW2a", "psW2b", "wrt_sb", "psH",
                  "wt_sb", "wbc")

        self.phase_gate()
        bias_sb = self.alloc("bias_sb", [P, DC], F32)
        S["CONST"].inc(sy.dma_start(out=bias_sb[:], in_=self.ffn_bias_pc), 16)
        bias_ld = S["CONST"].n
        self.ps8 = self.palloc("ps8f", [P, 4 * TL], F32)
        state = {"Rin": self.alloc("RinF", [P, 4 * TL], F32),
                 "Rst": self.alloc("RstF", [P, 2 * TL], F32),
                 "bias": bias_sb}

        def extra_lora(eg):
            te.wait_ge(S["CV"].h, hw_cv)
            te.wait_ge(S["CONST"].h, bias_ld)
            for sub in range(4):
                oc = eg * 4 + sub
                for th in range(2):
                    S["MM"].inc(te.matmul(
                        out=self.ps8[:, sub * TL + th * 512:
                                     sub * TL + (th + 1) * 512],
                        lhsT=bmT_sb[:, oc * P:(oc + 1) * P],
                        rhs=hwT[:, th * 512:(th + 1) * 512],
                        start=False, stop=True))

        ev_fn = self.resid_evac(self.R[1], self.R[2], state)

        def ev_dq(gi, eg, mmv):
            ev_fn(gi, eg, mmv)
            for _ in range(3):
                self.emit_dequant_task()

        self.proj_fm(self.wT_g["wb"], list(range(8)), ev_dq,
                     cc_wait=self.cc_w["wb"], extra_mm=extra_lora, tag="b0f",
                     tiled=True)
        self.free("RinF", "RstF", "ps8f", "bias_sb", "bmT_sb", "at_a", "hwT")
        self.free(*self.xb_names)

    def emit_l1_attn(self):
        sy, sc, ve, te, gp, S = self.sy, self.sc, self.ve, self.te, self.gp, self.S
        self.flush_dequant()
        self.free("dq_qraw", "dq_nib", "dq_nibb", "dq_w", "dq_zs", "dq_sb")
        self.phase_gate()
        bq_sb = self.alloc("bq_sb", [P, 24], F32)
        S["CONST"].inc(sy.dma_start(out=bq_sb[:], in_=self.bqkv_pc), 16)
        bq_ld = S["CONST"].n
        self.ps8 = self.palloc("ps81", [P, 4 * TL], F32)
        Qb1 = self.alloc("Qb1", [P, 8 * TL], BF16)
        kst = self.alloc("kst1", [P, 4 * TL], BF16)

        kst_sto = {}

        def evac_k(gi, eg, mmv):
            ve.wait_ge(S["MM"].h, mmv)
            ve.wait_ge(S["CONST"].h, bq_ld)
            if gi >= 1:
                ve.wait_ge(S["STK"].h, S["STK"].n)
            for sub in range(4):
                dq = eg * 4 + sub
                S["EV"].inc(ve.tensor_scalar(
                    out=kst[:, sub * TL:(sub + 1) * TL],
                    in0=self.ps8[:, sub * TL:(sub + 1) * TL],
                    scalar1=bq_sb[:, 8 + dq:9 + dq], scalar2=None,
                    op0=ALU.add))
            sc.wait_ge(S["EV"].h, S["EV"].n)
            for sub in range(4):
                dq = eg * 4 + sub
                S["STK"].inc(sc.dma_start(
                    out=self.l1k_in[dq * P:(dq + 1) * P, :],
                    in_=kst[:, sub * TL:(sub + 1) * TL]), 16)
            kst_sto[gi] = S["STK"].n

        self.proj_fm(self.wqkv_g, [0, 1], evac_k, col0=1024,
                     cc_wait=self.cc_wqkv, tag="k1")
        gp.wait_ge(S["STK"].h, S["STK"].n)
        S["CC"].inc(gp.collective_compute(
            "AllGather", ALU.bypass, replica_groups=G4,
            ins=[self.l1k_in], outs=[self.l1k_g]))
        self.cc_l1k = S["CC"].n

        vst = self.alloc("vst1", [P, 2 * 512], BF16)
        vsl = self.alloc("vslab1", [P, 4 * 512], BF16)
        self.slab_gate()
        bqv = self.alloc("bqv", [P, 2 * 512], F32)
        S["CONST"].inc(sy.dma_start(
            out=bqv[:, 0:512],
            in_=self.bqkv[2048:2560].unsqueeze(0).partition_broadcast(P)), 16)
        S["CONST"].inc(sy.dma_start(
            out=bqv[:, 512:1024],
            in_=self.bqkv[2560:3072].unsqueeze(0).partition_broadcast(P)), 16)
        bqv_ld = S["CONST"].n
        vs_marks = {}
        sto_marks = {}
        si = 0
        ev_mark = None
        for vg in range(2):
            for dc in range(DC):
                sl = self.SL[si % 4]
                if si >= 4:
                    sy.wait_ge(S["MM"].h, vs_marks[si - 4])
                sl.inc(sy.dma_start(
                    out=vsl[:, (si % 4) * 512:(si % 4) * 512 + 512],
                    in_=self.wqkv_g[dc * P:(dc + 1) * P,
                                    2048 + vg * 512:2048 + (vg + 1) * 512]), 16)
                te.wait_ge(sl.h, sl.n)
                if dc == 0:
                    te.wait_ge(S["CV"].h, self.ln_cv)
                    if ev_mark is not None:
                        te.wait_ge(S["EV"].h, ev_mark)
                for tc in range(8):
                    S["MM"].inc(te.matmul(
                        out=self.ps8[:, tc * 512:(tc + 1) * 512],
                        lhsT=self.xsl(dc, tc * P, P),
                        rhs=vsl[:, (si % 4) * 512:(si % 4) * 512 + 512],
                        start=(dc == 0), stop=(dc == DC - 1)))
                vs_marks[si] = S["MM"].n
                si += 1
            mmv = S["MM"].n
            for tc in range(8):
                ve.wait_ge(S["MM"].h, mmv)
                ve.wait_ge(S["CONST"].h, bqv_ld)
                u = vg * 8 + tc
                if u >= 2:
                    ve.wait_ge(S["STV"].h, S["STV"].n)
                S["EV"].inc(ve.tensor_tensor(
                    out=vst[:, (tc % 2) * 512:(tc % 2) * 512 + 512],
                    in0=self.ps8[:, tc * 512:(tc + 1) * 512],
                    in1=bqv[:, vg * 512:(vg + 1) * 512], op=ALU.add))
                sc.wait_ge(S["EV"].h, S["EV"].n)
                S["STV"].inc(sc.dma_start(
                    out=self.l1v_in[tc * P:(tc + 1) * P,
                                    vg * 512:(vg + 1) * 512],
                    in_=vst[:, (tc % 2) * 512:(tc % 2) * 512 + 512]), 16)
                sto_marks[u] = S["STV"].n
            ev_mark = S["EV"].n
        self.free("vslab1", "vst1", "bqv")
        gp.wait_ge(S["STV"].h, S["STV"].n)
        S["CC"].inc(gp.collective_compute(
            "AllGather", ALU.bypass, replica_groups=G4,
            ins=[self.l1v_in], outs=[self.l1v_g]))
        self.cc_l1v = S["CC"].n

        def evac_q(gi, eg, mmv):
            ve.wait_ge(S["MM"].h, mmv)
            ve.wait_ge(S["CONST"].h, bq_ld)
            for sub in range(4):
                dq = eg * 4 + sub
                S["EV"].inc(ve.tensor_scalar(
                    out=Qb1[:, dq * TL:(dq + 1) * TL],
                    in0=self.ps8[:, sub * TL:(sub + 1) * TL],
                    scalar1=bq_sb[:, dq:dq + 1], scalar2=None, op0=ALU.add))

        self.proj_fm(self.wqkv_g, [0, 1], evac_q, col0=0,
                     cc_wait=self.cc_wqkv, tag="q1")
        q_mm = S["MM"].n
        q_ev = S["EV"].n
        self.free("kst1", "ps81")
        self.free(*self.xb_names)

        self.phase_gate()
        Vf = self.alloc("Vf", [P, KB * 1025], BF16)
        Kf = self.alloc("Kf", [P, 8 * 4096], BF16)
        AT1 = self.alloc("AT1", [P, 8 * TL], BF16)
        Pt1 = self.alloc("Pt1", [P, 2 * P], BF16)
        atok1 = self.alloc("atok1", [P, TL], BF16)
        zin1 = self.alloc("zin1", [P, 1], F32)
        psS1 = [self.palloc("psS1a", [P, P], F32),
                self.palloc("psS1b", [P, P], F32)]
        psA = self.palloc("psA", [P, TL], F32)
        psZ = self.palloc("psZ", [P, 1], F32)
        psT1 = [self.palloc("psT1a", [P, P], BF16),
                self.palloc("psT1b", [P, P], BF16)]

        ve.wait_ge(S["MM"].h, q_mm)
        ve.wait_ge(S["EV"].h, q_ev)
        for kb in range(KB):
            S["CV"].inc(ve.memset(
                Vf[:, kb * 1025 + 1024:kb * 1025 + 1025], 1.0))
        vf_cv = S["CV"].n
        sy.wait_ge(S["CC"].h, self.cc_l1v)
        sy.wait_ge(S["CV"].h, vf_cv)
        for r in range(4):
            for dq in range(8):
                S["LD"].inc(sy.dma_start(
                    out=Kf[:, dq * 4096 + r * TL: dq * 4096 + (r + 1) * TL],
                    in_=self.l1k_g[r * D1 + dq * P: r * D1 + (dq + 1) * P, :]),
                    16)
            for kc in range(8):
                kb = r * 8 + kc
                S["LD"].inc(sy.dma_start(
                    out=Vf[:, kb * 1025:kb * 1025 + 1024],
                    in_=self.l1v_g[kb * P:(kb + 1) * P, :]), 16)
        kv_ld = S["LD"].n

        exp1 = {}
        pv1 = {}
        norm1 = {}
        tr1 = {}
        for qs in range(8):
            s1_marks = {}

            def emit_S1(kb):
                gidx = qs * KB + kb
                te.wait_ge(S["LD"].h, kv_ld)
                if gidx >= 2:
                    pg = gidx - 2
                    pkey = (pg // KB, pg % KB)
                    te.wait_ge(S["EX"].h, exp1[pkey])
                for dq in range(8):
                    S["MM"].inc(te.matmul(
                        out=psS1[kb % 2][:, :],
                        lhsT=Kf[:, dq * 4096 + kb * P: dq * 4096 + (kb + 1) * P],
                        rhs=Qb1[:, dq * TL + qs * P: dq * TL + (qs + 1) * P],
                        start=(dq == 0), stop=(dq == 7)))
                s1_marks[kb] = S["MM"].n

            def emit_exp_pv1(kb):
                gidx = qs * KB + kb
                sc.wait_ge(S["MM"].h, s1_marks[kb])
                if gidx >= 2:
                    pg = gidx - 2
                    pkey = (pg // KB, pg % KB)
                    sc.wait_ge(S["MM"].h, pv1[pkey])
                S["EX"].inc(sc.activation(
                    out=Pt1[:, (kb % 2) * P:(kb % 2) * P + P],
                    in_=psS1[kb % 2][:, :],
                    func=AF.Exp, scale=SCALE_L1, bias=self.fc[:, 2:3]))
                exp1[(qs, kb)] = S["EX"].n
                te.wait_ge(S["EX"].h, exp1[(qs, kb)])
                if kb == 0 and qs > 0:
                    te.wait_ge(S["CV"].h, norm1[qs - 1])
                    te.wait_ge(S["EV"].h, tr1[qs - 1])
                for vh in range(2):
                    S["MM"].inc(te.matmul(
                        out=psA[:, vh * 512:(vh + 1) * 512],
                        lhsT=Pt1[:, (kb % 2) * P:(kb % 2) * P + P],
                        rhs=Vf[:, kb * 1025 + vh * 512:
                               kb * 1025 + (vh + 1) * 512],
                        start=(kb == 0), stop=(kb == KB - 1)))
                S["MM"].inc(te.matmul(
                    out=psZ[:, :],
                    lhsT=Pt1[:, (kb % 2) * P:(kb % 2) * P + P],
                    rhs=Vf[:, kb * 1025 + 1024:kb * 1025 + 1025],
                    start=(kb == 0), stop=(kb == KB - 1)))
                pv1[(qs, kb)] = S["MM"].n

            for kb in range(KB):
                emit_S1(kb)
                if kb >= 1:
                    emit_exp_pv1(kb - 1)
            emit_exp_pv1(KB - 1)
            ve.wait_ge(S["MM"].h, pv1[(qs, KB - 1)])
            S["MIS"].inc(ve.reciprocal(out=zin1[:, :], in_=psZ[:, :]))
            ve.wait_ge(S["MIS"].h, S["MIS"].n)
            S["CV"].inc(ve.tensor_scalar(
                out=atok1[:], in0=psA[:, :], scalar1=zin1[:, 0:1],
                scalar2=None, op0=ALU.mult))
            norm1[qs] = S["CV"].n
            for dq in range(8):
                te.wait_ge(S["CV"].h, norm1[qs])
                if dq >= 2:
                    te.wait_ge(S["EV"].h, S["EV"].n)
                S["MM"].inc(te.transpose(
                    out=psT1[dq % 2][:, :],
                    in_=atok1[:, dq * P:(dq + 1) * P],
                    identity=self.id_sb[:]))
                sc.wait_ge(S["MM"].h, S["MM"].n)
                S["EV"].inc(sc.copy(
                    out=AT1[:, dq * TL + qs * P: dq * TL + (qs + 1) * P],
                    in_=psT1[dq % 2][:, :]))
            tr1[qs] = S["EV"].n
        at1_ev = S["EV"].n
        self.free("Kf", "Vf", "Pt1", "atok1", "zin1", "psS1a", "psS1b",
                  "psA", "psZ", "psT1a", "psT1b", "Qb1", "bq_sb")

        self.phase_gate()
        bo_sb = self.alloc("bo_sb", [P, DC], F32)
        S["CONST"].inc(sy.dma_start(out=bo_sb[:], in_=self.bo1_pc), 16)
        self.ps8 = self.palloc("ps8m", [P, 4 * TL], F32)
        state = {"Rin": self.alloc("RinM", [P, 4 * TL], F32),
                 "Rst": self.alloc("RstM", [P, 2 * TL], F32),
                 "bias": bo_sb}
        ev_fn = self.resid_evac(self.R[2], self.R[3], state)
        orig_xsl = self.xsl
        self.xsl = lambda dc, t0, width: AT1[:, dc * TL + t0: dc * TL + t0 + width]
        te.wait_ge(S["EV"].h, at1_ev)
        self.proj_fm(self.wo1_g, list(range(8)), ev_fn,
                     cc_wait=self.cc_wo1, tag="o1", kdc=8)
        self.xsl = orig_xsl
        self.free("RinM", "RstM", "ps8m", "bo_sb", "AT1")

    def emit_l1_ffn(self):
        sy, sc, ve, te, gp, S = self.sy, self.sc, self.ve, self.te, self.gp, self.S
        self.phase_gate()
        psG = self.palloc("psG", [P, 2 * 512], F32)
        psU = self.palloc("psU", [P, 2 * 512], F32)
        psD = self.palloc("psD", [P, 2 * 512], F32)
        sg = self.alloc("sg", [P, 2 * 512], F32)

        gu_marks = {}
        h_marks = {}
        if L1FFN_PART == 0:
            self.free("sg", "psG", "psU", "psD")
            return
        for th in range(2):
            Hh = self.alloc(f"Hh{th}", [P, ICN * 512], BF16)
            gsl = self.alloc(f"gsl{th}", [P, 2 * DC * P], BF16)
            usl = self.alloc(f"usl{th}", [P, 2 * DC * P], BF16)
            self.slab_gate()
            for ic in range(ICN):
                u = th * ICN + ic
                if u >= 2:
                    sy.wait_ge(S["MM"].h, gu_marks[u - 2])
                if u == 0:
                    sy.wait_ge(S["CC"].h, self.cc_gate)
                    sy.wait_ge(S["CC"].h, self.cc_up)
                slg = self.SL[ic % 2]
                slu = self.SL[2 + ic % 2]
                slg.inc(sy.dma_start(
                    out=gsl[:, (ic % 2) * DC * P:(ic % 2) * DC * P + DC * P]
                    .rearrange("p (c i) -> p c i", c=DC),
                    in_=self.gate_g[:, ic * P:(ic + 1) * P]
                    .rearrange("(c p) i -> p c i", p=P)), 16)
                slu.inc(sy.dma_start(
                    out=usl[:, (ic % 2) * DC * P:(ic % 2) * DC * P + DC * P]
                    .rearrange("p (c i) -> p c i", c=DC),
                    in_=self.up_g[:, ic * P:(ic + 1) * P]
                    .rearrange("(c p) i -> p c i", p=P)), 16)
                te.wait_ge(slg.h, slg.n)
                te.wait_ge(slu.h, slu.n)
                te.wait_ge(S["CV"].h, self.ln_cv)
                if u >= 2:
                    te.wait_ge(S["CV"].h, h_marks[u - 2])
                for dc in range(DC):
                    S["MM"].inc(te.matmul(
                        out=psG[:, (ic % 2) * 512:(ic % 2) * 512 + 512],
                        lhsT=gsl[:, (ic % 2) * DC * P + dc * P:
                                 (ic % 2) * DC * P + (dc + 1) * P],
                        rhs=self.xsl(dc, th * 512, 512),
                        start=(dc == 0), stop=(dc == DC - 1)))
                    S["MM"].inc(te.matmul(
                        out=psU[:, (ic % 2) * 512:(ic % 2) * 512 + 512],
                        lhsT=usl[:, (ic % 2) * DC * P + dc * P:
                                 (ic % 2) * DC * P + (dc + 1) * P],
                        rhs=self.xsl(dc, th * 512, 512),
                        start=(dc == 0), stop=(dc == DC - 1)))
                gu_marks[u] = S["MM"].n
                sc.wait_ge(S["MM"].h, gu_marks[u])
                if u >= 2:
                    sc.wait_ge(S["CV"].h, h_marks[u - 2])
                S["EX"].inc(sc.activation(
                    out=sg[:, (ic % 2) * 512:(ic % 2) * 512 + 512],
                    in_=psG[:, (ic % 2) * 512:(ic % 2) * 512 + 512],
                    func=AF.Silu))
                ve.wait_ge(S["EX"].h, S["EX"].n)
                S["CV"].inc(ve.tensor_tensor(
                    out=Hh[:, ic * 512:(ic + 1) * 512],
                    in0=sg[:, (ic % 2) * 512:(ic % 2) * 512 + 512],
                    in1=psU[:, (ic % 2) * 512:(ic % 2) * 512 + 512],
                    op=ALU.mult))
                h_marks[u] = S["CV"].n
            h_done = S["CV"].n
            if L1FFN_PART == 5 and th == 0:
                self.free(f"Hh{th}", f"gsl{th}", f"usl{th}")
                self.free(self.xb_names[th])
                self.phase_gate()
                continue
            if L1FFN_PART in (3, 4) and th == 1:
                self.free(f"Hh{th}", f"gsl{th}", f"usl{th}")
                self.free(self.xb_names[th])
                self.phase_gate()
                continue
            if L1FFN_PART < 2:
                self.free(f"Hh{th}", f"gsl{th}", f"usl{th}")
                self.free(self.xb_names[th])
                self.phase_gate()
                continue

            self.free(self.xb_names[th], f"gsl{th}", f"usl{th}")
            self.phase_gate()
            dsl = self.alloc(f"dsl{th}", [P, 2 * ICP * P], BF16)
            Rin = self.alloc(f"RinD{th}", [P, 2 * 512], F32)
            self.slab_gate()
            Rst = self.alloc(f"RstD{th}", [P, 2 * 512], F32)
            dn_marks = {}
            rst_marks = {}
            noc = 4 if L1FFN_PART == 3 else DC
            if L1FFN_PART == 4:
                noc = DC
            for oc in range(noc):
                if oc >= 2:
                    sy.wait_ge(S["MM"].h, dn_marks[oc - 2])
                if oc == 0 and th == 0:
                    sy.wait_ge(S["CC"].h, self.cc_down)
                    sy.wait_ge(S["CV"].h, h_done)
                sld = self.SL[oc % 2]
                for r in range(8):
                    sld.inc(sy.dma_start(
                        out=dsl[:, (oc % 2) * ICP * P + r * 11 * P:
                                (oc % 2) * ICP * P + (r + 1) * 11 * P]
                        .rearrange("p (c i) -> p c i", c=11),
                        in_=self.down_g[r * 11 * P:(r + 1) * 11 * P,
                                        oc * P:(oc + 1) * P]
                        .rearrange("(c p) i -> p c i", p=P)), 16)
                if oc >= 2:
                    sy.wait_ge(S["CV"].h, rst_marks.get(oc - 2, (None, 0))[1])
                rfs = self.RF[oc % 2]
                rfs.inc(sy.dma_start(
                    out=Rin[:, (oc % 2) * 512:(oc % 2) * 512 + 512],
                    in_=self.R[3][oc * P:(oc + 1) * P, th * 512:(th + 1) * 512]),
                    16)
                te.wait_ge(sld.h, sld.n)
                te.wait_ge(S["CV"].h, h_done)
                if oc >= 2:
                    te.wait_ge(S["EV"].h, dn_ev[oc - 2])
                for k in range(ICN):
                    col = ((k % 8) * 11 + (k // 8)) * P
                    S["MM"].inc(te.matmul(
                        out=psD[:, (oc % 2) * 512:(oc % 2) * 512 + 512],
                        lhsT=dsl[:, (oc % 2) * ICP * P + col:
                                 (oc % 2) * ICP * P + col + P],
                        rhs=Hh[:, k * 512:(k + 1) * 512],
                        start=(k == 0), stop=(k == ICN - 1)))
                dn_marks[oc] = S["MM"].n
                ve.wait_ge(S["MM"].h, dn_marks[oc])
                ve.wait_ge(rfs.h, rfs.n)
                if oc >= 2:
                    ve.wait_ge(S["STO"].h, S["STO"].n)
                S["CV"].inc(ve.tensor_tensor(
                    out=Rst[:, (oc % 2) * 512:(oc % 2) * 512 + 512],
                    in0=psD[:, (oc % 2) * 512:(oc % 2) * 512 + 512],
                    in1=Rin[:, (oc % 2) * 512:(oc % 2) * 512 + 512],
                    op=ALU.add))
                if oc == 0:
                    dn_ev = {}
                S["EV"].inc(ve.engine_nop())
                dn_ev[oc] = S["EV"].n
                sc.wait_ge(S["CV"].h, S["CV"].n)
                S["STO"].inc(sc.dma_start(
                    out=self.R[4][oc * P:(oc + 1) * P, th * 512:(th + 1) * 512],
                    in_=Rst[:, (oc % 2) * 512:(oc % 2) * 512 + 512]), 16)
                rst_marks[oc] = (S["STO"].n, S["CV"].n)
            self.free(f"Hh{th}", f"dsl{th}", f"RinD{th}", f"RstD{th}")
            self.phase_gate()
        self.free("sg", "psG", "psU", "psD")


def make_in_maps(inputs):
    asc = np.ascontiguousarray
    x = inputs["L0_attn_x"]
    maps = []
    for c in range(8):
        b, l = c // 4, c % 4
        cl = [min(k, ICN - 1) for k in range(ICP) if k % 8 == c]
        zg = inputs["L1_ffn_gate_qzeros"][4 * c:4 * (c + 1)]
        zu = inputs["L1_ffn_up_qzeros"][4 * c:4 * (c + 1)]
        zd = np.stack([inputs["L1_ffn_down_qzeros"][k] for k in cl], axis=0)
        zdp = np.zeros((11, 1376), np.int32)
        zdp[:, :512] = zd
        zall = np.concatenate([zg, zu, zdp], axis=0).astype(np.int32)
        sg = inputs["L1_ffn_gate_scales"][4 * c:4 * (c + 1)]
        su = inputs["L1_ffn_up_scales"][4 * c:4 * (c + 1)]
        sd = np.stack([inputs["L1_ffn_down_scales"][k] for k in cl], axis=0)
        sdp = np.zeros((11, IM), np.float32)
        sdp[:, :DM] = sd
        sall = np.concatenate([sg, su, sdp], axis=0).astype(np.float32)
        m = {
            "x_t": asc(x[b, l * TL:(l + 1) * TL, :].T),
            "wq_s": asc(inputs["L0_attn_w_q"][:, 512 * c:512 * (c + 1)]),
            "wk_s": asc(inputs["L0_attn_w_k"][:, 512 * c:512 * (c + 1)]),
            "wv_s": asc(inputs["L0_attn_w_v"][:, 512 * c:512 * (c + 1)]),
            "wo_s": asc(inputs["L0_attn_w_o"][:, 512 * c:512 * (c + 1)]),
            "wb_s": asc(inputs["L0_ffn_Wb"][:, 512 * c:512 * (c + 1)]),
            "ffn_bias_pc": asc(inputs["L0_ffn_bias"].reshape(DC, P).T),
            "wr": asc(inputs["L0_ffn_Wr"]),
            "a_flat": asc(inputs["L0_ffn_A"].reshape(P, DM)),
            "bm_flat": asc(inputs["L0_ffn_Bm"].reshape(8 * DM, 16)),
            "wqkv_s": asc(inputs["L1_attn_Wqkv"][512 * c:512 * (c + 1), :]),
            "wo1_s": asc(inputs["L1_attn_Wo"][P * c:P * (c + 1), :]),
            "bqkv": asc(inputs["L1_attn_bqkv"]),
            "bqkv_pc": asc(inputs["L1_attn_bqkv"].reshape(24, P).T),
            "bo1_pc": asc(inputs["L1_attn_bo"].reshape(DC, P).T),
            "gq_s": asc(inputs["L1_ffn_gate_qweight"][512 * c:512 * (c + 1), :]),
            "uq_s": asc(inputs["L1_ffn_up_qweight"][512 * c:512 * (c + 1), :]),
            "dq_s": asc(np.concatenate(
                [inputs["L1_ffn_down_qweight"][P * k:P * (k + 1), :]
                 for k in cl], axis=0)),
            "zall": asc(zall),
            "sall": asc(sall),
        }
        maps.append(m)
    return maps


def kernel(**inputs):
    b = Kb(upto=4)
    nc = b.build()
    in_maps = make_in_maps(inputs)
    res = run_bass_kernel_spmd(nc, in_maps, core_ids=list(range(8)))
    outs = [np.ascontiguousarray(res.results[c]["out_t"].T) for c in range(8)]
    full = np.stack([np.concatenate(outs[0:4], axis=0),
                     np.concatenate(outs[4:8], axis=0)], axis=0)
    return full.astype(np.float32)
